# version 08
# baseline (speedup 1.0000x reference)
import numpy as np
from contextlib import ExitStack

import concourse.bass as bass
import concourse.mybir as mybir
from concourse.bass_utils import run_bass_kernel_spmd

F32 = mybir.dt.float32
BF16 = mybir.dt.bfloat16
Alu = mybir.AluOpType
Act = mybir.ActivationFunctionType
AxX = mybir.AxisListType.X

DM, DI, N, FF, TAPS = 1024, 2048, 16, 4096, 4
Tc = 256
KDM = DM // 128
MDI = DI // 128
MIP = 2 * DI // 128
MFF = FF // 128
MDM = DM // 128
G = 16

TAPC = 0
LN_EPS = 1e-5
FFN_EPS = 1e-5
SCAN_EPS = 1e-6
DT_MIN = 1e-4


class Pool:

    def __init__(self, n):
        self.tok = [None] * n
        self.i = 0
        self.n = n

    def acquire(self):
        s = self.i % self.n
        self.i += 1
        t = self.tok[s]
        self.tok[s] = None
        return s, t

    def release(self, s, sem, val):
        self.tok[s] = (sem, val)


class Sched:
    def __init__(self):
        self.ops = {e: [] for e in ("tensor", "vector", "scalar", "sync", "gpsimd")}
        self.cnt = {}

    def op(self, eng, fn, waits=(), inc=None, inc_by=1):
        w = []
        for t in waits:
            if t is None:
                continue
            s, v = t
            assert v <= self.cnt.get(id(s), 0), f"wait on future event ({eng})"
            if v > 0:
                w.append((s, v))
        val = None
        if inc is not None:
            self.cnt[id(inc)] = self.cnt.get(id(inc), 0) + inc_by
            val = self.cnt[id(inc)]
        self.ops[eng].append((fn, w, inc, inc_by))
        return val


def build_program(S=4096, debug=False):
    NCH = S // Tc
    nc = bass.Bass("TRN2", target_bir_lowering=False, debug=False)
    dbg = {}

    x_d = nc.dram_tensor("x", [S, DM], F32, kind="ExternalInput").ap()
    w_ip = nc.dram_tensor("in_proj_w", [2 * DI, DM], F32, kind="ExternalInput").ap()
    w_cv = nc.dram_tensor("conv_w", [DI, DI, TAPS], F32, kind="ExternalInput").ap()
    b_cv = nc.dram_tensor("conv_b", [DI], F32, kind="ExternalInput").ap()
    w_dt = nc.dram_tensor("dt_w", [DI, DI], F32, kind="ExternalInput").ap()
    b_dt = nc.dram_tensor("dt_b", [DI], F32, kind="ExternalInput").ap()
    a_log = nc.dram_tensor("A_log", [DI, N], F32, kind="ExternalInput").ap()
    d_v = nc.dram_tensor("Dv", [DI], F32, kind="ExternalInput").ap()
    w_b = nc.dram_tensor("B_w", [N, DI], F32, kind="ExternalInput").ap()
    w_c = nc.dram_tensor("C_w", [N, DI], F32, kind="ExternalInput").ap()
    w_ow = nc.dram_tensor("out_w", [DM, DI], F32, kind="ExternalInput").ap()
    w_f1 = nc.dram_tensor("W1", [FF, DM], F32, kind="ExternalInput").ap()
    b_f1 = nc.dram_tensor("b1", [FF], F32, kind="ExternalInput").ap()
    w_f2 = nc.dram_tensor("W2", [DM, FF], F32, kind="ExternalInput").ap()
    b_f2 = nc.dram_tensor("b2", [DM], F32, kind="ExternalInput").ap()
    out_d = nc.dram_tensor("out", [S, DM], F32, kind="ExternalOutput").ap()

    ipT = nc.dram_tensor("ipT", [128, 2 * MIP * KDM * 128], BF16).ap()
    cvT = nc.dram_tensor("cvT", [128, 2 * MDI * TAPS * MDI * 128], BF16).ap()
    dtT = nc.dram_tensor("dtT", [128, 2 * MDI * MDI * 128], BF16).ap()
    owT = nc.dram_tensor("owT", [128, 2 * MDM * MDI * 128], BF16).ap()
    f1T = nc.dram_tensor("f1T", [128, MFF * KDM * 128], BF16).ap()
    f2T = nc.dram_tensor("f2T", [128, MDM * MFF * 128], BF16).ap()

    pos = [0]

    def seg(w):
        c = pos[0]
        pos[0] += w
        return c

    A_SX = seg(2 * 2 * DM)
    A_XNT = seg(2 * DM)
    A_OUT = seg(2 * DM)
    A_AT = seg(2 * DM)
    A_XNF = seg(2 * 2 * KDM * 128)
    XCW = 130
    A_XC = seg(2 * MDI * XCW)
    A_Z = seg(MDI * Tc)
    A_UH = seg(MDI * 128)
    A_UL = seg(MDI * 128)
    A_DT = seg(MDI * Tc)
    A_DTU = seg(MDI * Tc)
    A_Y = seg(MDI * Tc)
    A_WR = seg(2 * G * 64)
    A_DA = seg(2 * Tc)
    A_B = seg(2 * Tc)
    A_SS = seg(2 * Tc)
    A_P = seg(2 * Tc)
    A_ST = seg(Tc)
    A_EPS = seg(1)
    A_ASB = seg(Tc)
    A_BCW = seg(2 * 2 * MDI * 8)
    A_SBC = seg(2 * Tc)
    A_SEL = seg(N * 64)
    A_CB = seg(MDI)
    A_DB = seg(MDI)
    A_DV = seg(MDI)
    A_B1 = seg(MFF)
    A_B2 = seg(MDM)
    A_ID = seg(128)
    A_IDB = seg(64)
    A_ONE = seg(1)
    A_ONR = seg(128)
    A_STAT = seg(16)
    A_GAM = seg(8)
    ACOLS = ((pos[0] + 127) // 128) * 128
    assert ACOLS <= 49152, f"arena too big: {ACOLS}"

    P_NAT = A_Z
    assert A_DTU - A_Z >= 8192 - 4096
    assert A_Y - A_Z >= 8192, "P_NAT needs 8192 contiguous cols"
    P_NH = A_DTU
    sch = Sched()
    P_NHB = A_DTU
    P_NLB = A_Y

    with ExitStack() as est:
        en = est.enter_context
        ar_h = en(nc.sbuf_tensor("arena", [128, ACOLS], F32))
        ps_h = en(nc.psum_tensor("ps", [128, 4096], F32))
        sD = [en(nc.semaphore(f"sD{i}")) for i in range(6)]
        sW = [en(nc.semaphore(f"sW{i}")) for i in range(2)]
        sT = en(nc.semaphore("sT"))
        sS = en(nc.semaphore("sS"))
        sV = en(nc.semaphore("sV"))
        sG = en(nc.semaphore("sG"))
        sTAP = en(nc.semaphore("sTAP"))
        block = en(nc.Block())

        def a(c0, w, p0=0, pn=128):
            return ar_h[p0:p0 + pn, c0:c0 + w]

        def ab(c0, wbf, p0=0, pn=128):
            return ar_h[p0:p0 + pn, c0:c0 + wbf // 2].bitcast(BF16)

        def bank(k, c0=0, w=512, p0=0, pn=128):
            return ps_h[p0:p0 + pn, k * 512 + c0:k * 512 + c0 + w]

        def bankb(k, c0bf=0, wbf=1024, p0=0, pn=128):
            return ps_h[p0:p0 + pn,
                        k * 512 + c0bf // 2:
                        k * 512 + (c0bf + wbf) // 2].bitcast(BF16)

        TPB, PPB, REPB = (0, 1), (4, 5, 6, 7), (2, 3)
        tp_pool, pp_pool, rep_pool = Pool(2), Pool(4), Pool(2)
        da_pool, hf_pool = Pool(2), Pool(2)
        HFC = (A_B, A_SS)

        op = sch.op

        dseq, dcnt = [0], {}
        wseq, wcnt = [0], {}

        def dload(fn, waits=()):
            i = dseq[0] % len(sD)
            dseq[0] += 1
            w = list(waits)
            prev = dcnt.get(i, 0)
            if prev:
                w.append((sD[i], prev))
            v = op("sync", fn, waits=w, inc=sD[i], inc_by=16)
            dcnt[i] = v
            return (sD[i], v)

        def dstore(fn, waits=()):
            i = wseq[0] % len(sW)
            wseq[0] += 1
            w = list(waits)
            prev = wcnt.get(i, 0)
            if prev:
                w.append((sW[i], prev))
            v = op("gpsimd", fn, waits=w, inc=sW[i], inc_by=16)
            wcnt[i] = v
            return (sW[i], v)

        def dall():
            return [(sW[i], wcnt[i]) for i in range(len(sW)) if wcnt.get(i)]

        def tap(name, c0, w, after, pn=128, bf=False):
            if not debug:
                return None
            if bf:
                t = nc.dram_tensor("dbg_" + name, [pn, 2 * w], BF16,
                                   kind="ExternalOutput").ap()
                src = ab(c0, 2 * w, pn=pn)
            else:
                t = nc.dram_tensor("dbg_" + name, [pn, w], F32,
                                   kind="ExternalOutput").ap()
                src = a(c0, w, pn=pn)
            dbg[name] = t
            return op("gpsimd", lambda e, t=t, src=src: e.dma_start(
                out=t, in_=src), waits=[after], inc=sTAP, inc_by=16)

        g_iota = op("gpsimd", lambda e: e.iota(
            out=a(A_ID, 128), pattern=[[-1, 128]], base=0, channel_multiplier=1,
            allow_small_or_imprecise_dtypes=True), inc=sG)
        v_id = op("vector", lambda e: e.tensor_scalar(
            out=a(A_ID, 128), in0=a(A_ID, 128), scalar1=0.0, scalar2=None,
            op0=Alu.is_equal), waits=[(sG, g_iota)], inc=sV)
        v_idb = op("vector", lambda e: e.tensor_copy(
            out=ab(A_IDB, 128), in_=a(A_ID, 128)), waits=[(sV, v_id)], inc=sV)
        g_sel = op("gpsimd", lambda e: e.iota(
            out=a(A_AT, N * 128).rearrange("p (n j) -> p n j", j=128),
            pattern=[[-1, N], [0, 128]], base=0, channel_multiplier=1,
            allow_small_or_imprecise_dtypes=True), inc=sG)
        v_self = op("vector", lambda e: e.tensor_scalar(
            out=ab(A_SEL, N * 128), in0=a(A_AT, N * 128),
            scalar1=0.0, scalar2=None, op0=Alu.is_equal),
            waits=[(sG, g_sel)], inc=sV)
        op("vector", lambda e: e.memset(a(A_SBC, 2 * Tc), 0.0), inc=sV)
        op("vector", lambda e: e.memset(a(A_ONE, 1), 1.0), inc=sV)
        v_onr = op("vector", lambda e: e.memset(a(A_ONR, 128, pn=1), 1.0), inc=sV)
        op("vector", lambda e: e.memset(a(A_XC, 2 * MDI * XCW), 0.0), inc=sV)
        op("vector", lambda e: e.memset(a(A_ST, Tc), 0.0), inc=sV)
        op("vector", lambda e: e.memset(a(A_EPS, 1), SCAN_EPS), inc=sV)

        btoks = []
        for src, dst, wd in ((b_cv, A_CB, MDI), (b_dt, A_DB, MDI),
                             (d_v, A_DV, MDI), (b_f1, A_B1, MFF),
                             (b_f2, A_B2, MDM)):
            btoks.append(dload(lambda e, src=src, dst=dst, wd=wd: e.dma_start(
                out=a(dst, wd), in_=src.rearrange("(c p) -> p c", p=128),
                allow_slow_non_contiguous=True)))
        d_alog = dload(lambda e: e.dma_start(
            out=a(A_ASB, Tc).rearrange("p (c n) -> p c n", n=N),
            in_=a_log.rearrange("(c p) n -> p c n", p=128)))
        s_exp = op("scalar", lambda e: e.activation(
            out=a(A_ASB, Tc), in_=a(A_ASB, Tc), func=Act.Exp),
            waits=[d_alog] + btoks, inc=sS)
        v_asb = op("vector", lambda e: e.tensor_scalar(
            out=a(A_ASB, Tc), in0=a(A_ASB, Tc), scalar1=-1.0, scalar2=None,
            op0=Alu.mult), waits=[(sS, s_exp)] + btoks, inc=sV)
        tap("asb", A_ASB, Tc, (sV, v_asb))

        nat_tok = [None]
        str_tok = [None]
        n_prep_stores = [0]

        last = None
        for wi, wsrc in enumerate((w_b, w_c)):
            d0 = dload(lambda e, w=wsrc: e.dma_start(
                out=a(P_NAT, DI, pn=N), in_=w), waits=[nat_tok[0]])
            sh = op("scalar", lambda e: e.activation(
                out=ab(P_NHB, DI, pn=N), in_=a(P_NAT, DI, pn=N),
                func=Act.Copy), waits=[d0], inc=sS)
            sl = op("vector", lambda e: e.tensor_sub(
                out=ab(P_NLB, DI, pn=N), in0=a(P_NAT, DI, pn=N),
                in1=ab(P_NHB, DI, pn=N)), waits=[(sS, sh)], inc=sV)
            for hl, srcb in ((0, P_NHB), (1, P_NLB)):
                for half in range(2):
                    slot, tok = tp_pool.acquire()
                    for j in range(8):
                        k = half * 8 + j
                        last = op("tensor",
                                  lambda e, k=k, j=j, slot=slot, srcb=srcb:
                                  e.matmul(
                                      bankb(TPB[slot], j * N, N),
                                      ab(srcb + k * 64, 128, pn=N),
                                      ab(A_IDB, N, pn=N),
                                      start=(j == 0), stop=(j == 7),
                                      is_transpose=True),
                                  waits=([(sV, sl), (sV, v_idb), tok]
                                         if j == 0 else []),
                                  inc=(sT if j == 7 else None))
                    ev = op("scalar",
                            lambda e, slot=slot, wi=wi, hl=hl, half=half:
                            e.activation(
                                out=ab(A_BCW +
                                       ((wi * 2 + hl) * MDI + half * 8) * 8,
                                       8 * N),
                                in_=bankb(TPB[slot], 0, 8 * N),
                                func=Act.Copy),
                            waits=[(sT, last)], inc=sS)
                    tp_pool.release(slot, sS, ev)
            nat_tok[0] = (sT, last)

        def prep_weight(nat_src_fn, nblocks, ktiles, scratch, width,
                        quant=None, strided_taps=False):
            for m in range(nblocks):
                d0 = dload(lambda e, m=m: e.dma_start(
                    out=a(P_NAT, width), in_=nat_src_fn(m)),
                    waits=[nat_tok[0]])
                if quant is None:
                    sh = op("scalar", lambda e, width=width: e.activation(
                        out=ab(P_NHB, width), in_=a(P_NAT, width),
                        func=Act.Copy), waits=[d0], inc=sS)
                    sl = op("vector", lambda e, width=width: e.tensor_sub(
                        out=ab(P_NLB, width), in0=a(P_NAT, width),
                        in1=ab(P_NHB, width)), waits=[(sS, sh)], inc=sV)
                    nat_tok[0] = (sV, sl)
                    gate = (sV, sl)
                    ntile2 = 2 * ktiles
                else:
                    vq = quant(m, d0)
                    nat_tok[0] = (sV, vq)
                    gate = (sV, vq)
                    ntile2 = ktiles
                last_ev = None
                last = None
                nb = (ntile2 + 7) // 8
                for bt in range(nb):
                    slot, tok = tp_pool.acquire()
                    nin = min(8, ntile2 - bt * 8)
                    for j in range(nin):
                        t2 = bt * 8 + j
                        if quant is None:
                            k, hl = t2 // 2, t2 % 2
                            srcb = P_NHB if hl == 0 else P_NLB
                        else:
                            k, srcb = t2, P_NLB
                        if strided_taps:
                            tap_, i = k // MDI, k % MDI
                            src = ab(srcb, width).rearrange(
                                "p (i t) -> p i t", t=TAPS)[
                                :, i * 128:(i + 1) * 128, tap_]
                        else:
                            src = ab(srcb + k * 64, 128)
                        w0 = [tok] + ([gate, (sV, v_idb)]
                                      if (bt == 0 and j == 0) else [])
                        last = op("tensor",
                                  lambda e, src=src, slot=slot, j=j, nin=nin:
                                  e.matmul(
                                      bankb(TPB[slot], j * 128, 128), src,
                                      ab(A_IDB, 128), start=(j == 0),
                                      stop=(j == nin - 1), is_transpose=True),
                                  waits=(w0 if j == 0 else []),
                                  inc=(sT if j == nin - 1 else None))
                    ev = op("scalar",
                            lambda e, slot=slot, bt=bt, nin=nin: e.activation(
                                out=ab(P_STR + bt * 512, nin * 128),
                                in_=bankb(TPB[slot], 0, nin * 128),
                                func=Act.Copy),
                            waits=[(sT, last)] + ([str_tok[0]]
                                                  if bt == 0 else []),
                            inc=sS)
                    tp_pool.release(slot, sS, ev)
                    last_ev = ev
                if quant is None:
                    nat_tok[0] = (sT, last)
                str_tok[0] = dstore(
                    lambda e, m=m, nt=ntile2, scr=scratch: e.dma_start(
                        out=scr[:, m * nt * 128:(m + 1) * nt * 128],
                        in_=ab(P_STR, nt * 128)),
                    waits=[(sS, last_ev)])
                n_prep_stores[0] += 1

        P_STR = A_SX

        prep_weight(lambda m: w_ip[m * 128:(m + 1) * 128, :], MIP, KDM, ipT, DM)
        prep_weight(lambda m: w_cv[m * 128:(m + 1) * 128, :, :].rearrange(
            "p i t -> p (i t)"), MDI, TAPS * MDI, cvT, DI * TAPS,
            strided_taps=True)
        prep_weight(lambda m: w_dt[m * 128:(m + 1) * 128, :], MDI, MDI, dtT, DI)
        prep_weight(lambda m: w_ow[m * 128:(m + 1) * 128, :], MDM, MDI, owT, DI)

        def gamma_pass(wsrc, nblocks, width, gcol):
            op("vector", lambda e: e.memset(a(gcol, 1), 0.0), inc=sV)
            rv = None
            for m in range(nblocks):
                d0 = dload(lambda e, m=m, w=wsrc: e.dma_start(
                    out=a(P_NAT, width), in_=w[m * 128:(m + 1) * 128, :]),
                    waits=[nat_tok[0]])
                gv = op("vector", lambda e, width=width: e.reduce_sum(
                    out=a(gcol + 1, 1), in_=a(P_NAT, width), axis=AxX,
                    apply_absolute_value=True), waits=[d0], inc=sV)
                rv = op("vector", lambda e: e.tensor_add(
                    out=a(gcol, 1), in0=a(gcol, 1), in1=a(gcol + 1, 1)),
                    waits=[(sV, gv)], inc=sV)
                nat_tok[0] = (sV, rv)
            mm = op("tensor", lambda e: e.matmul(
                bank(PPB[0], 0, 1, pn=1), a(gcol, 1), a(A_ONE, 1),
                start=True, stop=True), waits=[(sV, rv)], inc=sT)
            e1 = op("scalar", lambda e: e.activation(
                out=a(gcol + 2, 1, pn=1), in_=bank(PPB[0], 0, 1, pn=1),
                func=Act.Copy), waits=[(sT, mm)], inc=sS)
            nelem = float(nblocks * 128 * width)
            w1_ = op("vector", lambda e, nelem=nelem: e.tensor_scalar(
                out=a(gcol + 3, 1, pn=1), in0=a(gcol + 2, 1, pn=1),
                scalar1=1.0 / nelem, scalar2=FFN_EPS, op0=Alu.mult,
                op1=Alu.add), waits=[(sS, e1)], inc=sV)
            w2_ = op("vector", lambda e: e.reciprocal(
                out=a(gcol + 4, 1, pn=1), in_=a(gcol + 3, 1, pn=1)),
                waits=[(sV, w1_)], inc=sV)
            w3_ = op("vector", lambda e: e.tensor_mul(
                out=a(gcol + 5, 1, pn=1), in0=a(gcol + 3, 1, pn=1),
                in1=a(gcol + 4, 1, pn=1)), waits=[(sV, w2_)], inc=sV)
            w4_ = op("vector", lambda e: e.tensor_scalar(
                out=a(gcol + 5, 1, pn=1), in0=a(gcol + 5, 1, pn=1),
                scalar1=-1.0, scalar2=2.0, op0=Alu.mult, op1=Alu.add),
                waits=[(sV, w3_)], inc=sV)
            v2 = op("vector", lambda e: e.tensor_mul(
                out=a(gcol + 4, 1, pn=1), in0=a(gcol + 4, 1, pn=1),
                in1=a(gcol + 5, 1, pn=1)), waits=[(sV, w4_)], inc=sV)
            mm2 = op("tensor", lambda e: e.matmul(
                bank(PPB[1], 0, 1), a(A_ONR, 128, pn=1), a(gcol + 4, 1, pn=1),
                start=True, stop=True), waits=[(sV, v2), (sV, v_onr)], inc=sT)
            e2 = op("scalar", lambda e: e.activation(
                out=a(gcol + 6, 1), in_=bank(PPB[1], 0, 1), func=Act.Copy),
                waits=[(sT, mm2)], inc=sS)
            return gcol + 6, e2

        qtm_tok = [None]

        def mk_quant(rc, rev, g1c, g2c, width):
            def quant(m, d0):
                op("vector", lambda e: e.tensor_scalar(
                    out=a(g1c, width), in0=a(P_NAT, width), scalar1=a(rc, 1),
                    scalar2=0.5, op0=Alu.mult, op1=Alu.is_gt),
                    waits=[d0, (sS, rev), qtm_tok[0]], inc=sV)
                op("vector", lambda e: e.tensor_scalar(
                    out=a(g2c, width), in0=a(P_NAT, width), scalar1=a(rc, 1),
                    scalar2=-0.5, op0=Alu.mult, op1=Alu.is_lt), inc=sV)
                vq = op("vector", lambda e: e.tensor_sub(
                    out=ab(P_NLB, width), in0=a(g1c, width), in1=a(g2c, width)),
                    inc=sV)
                qtm_tok[0] = (sV, vq)
                return vq
            return quant

        r1c, r1ev = gamma_pass(w_f1, MFF, DM, A_GAM)
        prep_weight(lambda m: w_f1[m * 128:(m + 1) * 128, :], MFF, KDM, f1T, DM,
                    quant=mk_quant(r1c, r1ev, A_AT, A_AT + 1024, DM))
        r2c, r2ev = gamma_pass(w_f2, MDM, FF, A_GAM)
        prep_weight(lambda m: w_f2[m * 128:(m + 1) * 128, :], MDM, MFF, f2T, FF,
                    quant=mk_quant(r2c, r2ev, A_DTU, A_AT, FF))
        v_rz = op("vector", lambda e: e.memset(a(A_XC, 2 * MDI * XCW), 0.0),
                  inc=sV)

        PREP_STORES = n_prep_stores[0]
        prep_store_toks = dall()

        conv_last, ffn1_last, ffn2_last = {}, {}, {}
        out_mm_last, ft_last, fev_last, ym_ev, a_ev = {}, {}, {}, {}, {}
        out_store_tok = {}
        xload, dtp_ev, bc_ev, bc_ev_l, rep_last, xnf_ready = {}, {}, {}, {}, {}, {}
        tr_last, xc_all, dt_all, relu_ev, ucv_all = {}, {}, {}, {}, {}
        y_last = {}
        ypend = {0: None, 1: None}
        wgrp = [0]
        ring_last = {}
        utap, dttap, ymtap = [None], [None], [None]

        def wload(scratch, t0, ntiles=G, extra=()):
            gi = wgrp[0]
            wgrp[0] += 1
            slot = gi % 2
            waits = list(extra)
            if gi >= 2:
                waits.append((sT, ring_last[gi - 2]))
            dv = dload(lambda e, scr=scratch, t0=t0, nt=ntiles, slot=slot:
                       e.dma_start(
                           out=ab(A_WR + slot * G * 64, nt * 128),
                           in_=scr[:, t0 * 128:(t0 + nt) * 128]),
                       waits=waits)
            return gi, slot, dv

        def wtile(slot, i):
            return ab(A_WR + slot * G * 64 + i * 64, 128)

        def xnfh(c, k):
            return ab(A_XNF + (c % 2) * 2 * KDM * 128 + k * 128, Tc)

        def xnfl(c, k):
            return ab(A_XNF + (c % 2) * 2 * KDM * 128 + KDM * 128 + k * 128,
                      Tc)

        def xch(i, t0, w):
            return ab(A_XC + i * XCW, 2 * XCW)[:, t0:t0 + w]

        def xcl(i, t0, w):
            return ab(A_XC + MDI * XCW + i * XCW, 2 * XCW)[:, t0:t0 + w]

        def uh(k):
            return ab(A_UH + k * 128, Tc)

        def ul(k):
            return ab(A_UL + k * 128, Tc)

        def yh(k):
            return ab(A_DTU + k * 128, Tc)

        def yl(k):
            return ab(A_DTU + 2048 + k * 128, Tc)

        def hh(k):
            return ab(A_Z + k * 128, Tc)

        def hl(k):
            return ab(A_DT + k * 128, Tc)

        def emit_xload(c, extra=()):
            xload[c] = dload(lambda e, c=c: e.dma_start(
                out=a(A_SX + (c % 2) * 2 * DM, 2 * DM).rearrange(
                    "p (t d) -> p t d", d=DM),
                in_=x_d[c * Tc:(c + 1) * Tc, :].rearrange(
                    "(t p) d -> p t d", p=128)),
                waits=list(extra))

        def emit_ln(c):
            sx = A_SX + (c % 2) * 2 * DM
            hw = [xload[c]]
            if c == 0:
                hw.append((sV, v_rz))
            if c > 0:
                hw.append((sT, conv_last[c - 1]))
            if c > 1:
                hw.append((sT, ft_last[c - 2]))
            op("vector", lambda e: e.tensor_copy(
                out=ab(A_XC, 4 * MDI * XCW).rearrange(
                    "p (m w) -> p m w", w=2 * XCW)[:, :, 0:3],
                in_=ab(A_XC, 4 * MDI * XCW).rearrange(
                    "p (m w) -> p m w", w=2 * XCW)[:, :, Tc:Tc + 3]),
                waits=hw, inc=sV)
            vv = None
            for tt in range(2):
                mv = op("vector", lambda e, tt=tt, sx=sx: e.reduce_sum(
                    out=a(A_STAT + 6 + tt, 1), in_=a(sx + tt * DM, DM),
                    axis=AxX), inc=sV)
                nv = op("vector", lambda e, tt=tt: e.tensor_scalar(
                    out=a(A_STAT + 8 + tt, 1), in0=a(A_STAT + 6 + tt, 1),
                    scalar1=-1.0 / DM, scalar2=None, op0=Alu.mult),
                    waits=[(sV, mv)], inc=sV)
                xw = [(sV, nv)]
                if tt == 0 and c > 0:
                    xw.append((sT, tr_last[c - 1]))
                op("vector", lambda e, tt=tt, sx=sx: e.tensor_scalar(
                    out=a(A_XNT + tt * DM, DM), in0=a(sx + tt * DM, DM),
                    scalar1=a(A_STAT + 8 + tt, 1), scalar2=None, op0=Alu.add),
                    waits=xw, inc=sV)
                op("vector", lambda e, tt=tt: e.tensor_mul(
                    out=a(A_AT + tt * DM, DM), in0=a(A_XNT + tt * DM, DM),
                    in1=a(A_XNT + tt * DM, DM)), inc=sV)
                qv = op("vector", lambda e, tt=tt: e.reduce_sum(
                    out=a(A_STAT + 10 + tt, 1), in_=a(A_AT + tt * DM, DM),
                    axis=AxX), inc=sV)
                vv = op("vector", lambda e, tt=tt: e.tensor_scalar(
                    out=a(A_STAT + tt, 1), in0=a(A_STAT + 10 + tt, 1),
                    scalar1=1.0 / DM, scalar2=LN_EPS, op0=Alu.mult,
                    op1=Alu.add), waits=[(sV, qv)], inc=sV)
            sq = op("scalar", lambda e: e.activation(
                out=a(A_STAT + 2, 2), in_=a(A_STAT, 2), func=Act.Sqrt),
                waits=[(sV, vv)], inc=sS)
            rcv = op("vector", lambda e: e.reciprocal(
                out=a(A_STAT + 4, 2), in_=a(A_STAT + 2, 2)), waits=[(sS, sq)],
                inc=sV)
            for tt in range(2):
                op("vector", lambda e, tt=tt: e.tensor_scalar(
                    out=a(A_XNT + tt * DM, DM), in0=a(A_XNT + tt * DM, DM),
                    scalar1=a(A_STAT + 4 + tt, 1), scalar2=None, op0=Alu.mult),
                    waits=([(sV, rcv)] if tt == 0 else []), inc=sV)
            vh = op("vector", lambda e: e.tensor_copy(
                out=ab(A_AT, 2 * DM), in_=a(A_XNT, 2 * DM)), inc=sV)
            vxn = op("vector", lambda e: e.tensor_sub(
                out=ab(A_AT + DM, 2 * DM), in0=a(A_XNT, 2 * DM),
                in1=ab(A_AT, 2 * DM)), waits=[(sV, vh)], inc=sV)
            a_ev[c] = vxn
            if c == TAPC:
                tap("xnt", A_XNT, 2 * DM, (sV, a_ev[c]))
                tap("stat", A_STAT, 16, (sV, rcv))

        pend = {}

        def acq(pool, cur_slice):
            s, t = pool.acquire()
            key = (id(pool), s)
            if t is None and key in pend:
                pend.pop(key)[1]()
                t = pool.tok[s]
                pool.tok[s] = None
            return s, t

        def defer(pool, s, sl, fn):
            pend[(id(pool), s)] = (sl, fn)

        def flush_pend(max_slice):
            for key in list(pend):
                if pend[key][0] <= max_slice:
                    pend.pop(key)[1]()

        def build_units(c):
            units = []

            def mk_tp(bt):
                def go(cur_slice=0):
                    hlv, tt = bt // 2, bt % 2
                    srcb = A_AT + hlv * DM + tt * 512
                    slot, tok = tp_pool.acquire()
                    last = None
                    for j in range(8):
                        last = op("tensor",
                                  lambda e, j=j, slot=slot, srcb=srcb: e.matmul(
                                      bankb(TPB[slot], j * 128, 128),
                                      ab(srcb + j * 64, 128), ab(A_IDB, 128),
                                      start=(j == 0), stop=(j == 7),
                                      is_transpose=True),
                                  waits=([(sV, a_ev[c]), tok] if j == 0
                                         else []),
                                  inc=(sT if j == 7 else None))
                    tr_last[c] = last
                    ew = [(sT, last)]
                    if bt == 0 and c > 1:
                        ew.append((sT, ffn1_last[c - 2]))
                    dstb = (A_XNF + (c % 2) * 2 * KDM * 128 + hlv * KDM * 128)
                    ev = op("scalar",
                            lambda e, slot=slot, dstb=dstb, tt=tt:
                            e.activation(
                                out=ab(dstb, KDM * Tc).rearrange(
                                    "p (dc t) -> p dc t", t=Tc)[
                                    :, :, tt * 128:(tt + 1) * 128],
                                in_=bankb(TPB[slot], 0, 1024).rearrange(
                                    "p (j t) -> p j t", t=128),
                                func=Act.Copy), waits=ew, inc=sS)
                    tp_pool.release(slot, sS, ev)
                    xnf_ready[c] = ev
                    return None
                return go
            for bt in range(4):
                units.append(mk_tp(bt))

            def mk_ip(m):
                def go(cur_slice=0):
                    gi, slot, dv = wload(ipT, m * 2 * KDM)
                    pslot, ptok = acq(pp_pool, cur_slice)
                    lastv = None
                    first = [True]
                    for xi, wj in ((0, 0), (1, 0), (0, 1)):
                        for k in range(KDM):
                            w0 = []
                            if first[0]:
                                w0 = [dv, ptok, (sS, xnf_ready[c])]
                                first[0] = False
                            xs = xnfh(c, k) if xi == 0 else xnfl(c, k)
                            st = (xi == 0 and wj == 0 and k == 0)
                            sp = (xi == 0 and wj == 1 and k == KDM - 1)
                            lastv = op("tensor",
                                       lambda e, k=k, wj=wj, xs=xs, slot=slot,
                                       pslot=pslot, st=st, sp=sp: e.matmul(
                                           bank(PPB[pslot], 0, Tc),
                                           wtile(slot, k * 2 + wj), xs,
                                           start=st, stop=sp),
                                       waits=w0, inc=(sT if sp else None))
                    ring_last[gi] = lastv

                    def evac():
                        ew = [(sT, lastv), (sV, a_ev[c])]
                        ev = op("scalar", lambda e, m=m, pslot=pslot:
                                e.activation(
                                    out=xch(m, 3, Tc),
                                    in_=bank(PPB[pslot], 0, Tc),
                                    func=Act.Copy), waits=ew, inc=sS)
                        lv = op("vector", lambda e, m=m, pslot=pslot:
                                e.tensor_sub(
                                    out=xcl(m, 3, Tc),
                                    in0=bank(PPB[pslot], 0, Tc),
                                    in1=xch(m, 3, Tc)), waits=[(sS, ev)],
                                inc=sV)
                        pp_pool.release(pslot, sV, lv)
                        xc_all[c] = lv
                    defer(pp_pool, pslot, cur_slice, evac)
                    return None
                return go
            for m in range(MDI):
                units.append(mk_ip(m))

            cvst = {}

            def mk_cv(m, tg, gg):
                def go(cur_slice=0):
                    gi, slot, dv = wload(
                        cvT, ((m * TAPS + tg) * MDI + gg * 8) * 2)
                    if tg == 0 and gg == 0:
                        cvst[m] = acq(pp_pool, cur_slice)
                    pslot, ptok = cvst[m]
                    first_mm = (tg == 0 and gg == 0)
                    lastv = None
                    first = [True]
                    for xi, wj in ((0, 0), (1, 0), (0, 1)):
                        for i8 in range(8):
                            i = gg * 8 + i8
                            w0 = []
                            if first[0]:
                                w0 = [dv]
                                if first_mm:
                                    w0.append(ptok)
                                    if m == 0:
                                        w0.append((sV, xc_all[c]))
                                first[0] = False
                            xs = (xch(i, tg, Tc) if xi == 0
                                  else xcl(i, tg, Tc))
                            st = (first_mm and xi == 0 and wj == 0
                                  and i8 == 0)
                            sp = (tg == TAPS - 1 and gg == 1 and xi == 0
                                  and wj == 1 and i8 == 7)
                            lastv = op("tensor",
                                       lambda e, i8=i8, wj=wj, xs=xs,
                                       slot=slot, pslot=pslot, st=st, sp=sp:
                                       e.matmul(
                                           bank(PPB[pslot], 0, Tc),
                                           wtile(slot, i8 * 2 + wj), xs,
                                           start=st, stop=sp),
                                       waits=w0,
                                       inc=(sT if (xi == 0 and wj == 1
                                                   and i8 == 7) else None))
                    ring_last[gi] = lastv
                    if not (tg == TAPS - 1 and gg == 1):
                        return None
                    conv_last[c] = lastv

                    def evac():
                        ew = [(sT, lastv)]
                        if m == 0 and c > 0:
                            ew.append((sV, dtp_ev[c - 1]))
                            if debug and c == TAPC + 1 and utap[0] is not None:
                                ew.append((sTAP, utap[0]))
                        ev = op("scalar", lambda e, m=m, pslot=pslot:
                                e.activation(
                                    out=uh(m), in_=bank(PPB[pslot], 0, Tc),
                                    func=Act.Identity, bias=a(A_CB + m, 1)),
                                waits=ew, inc=sS)
                        lv = op("vector", lambda e, m=m, pslot=pslot:
                                e.scalar_tensor_tensor(
                                    out=ul(m), in0=bank(PPB[pslot], 0, Tc),
                                    scalar=a(A_CB + m, 1), in1=uh(m),
                                    op0=Alu.add, op1=Alu.subtract),
                                waits=[(sS, ev)], inc=sV)
                        pp_pool.release(pslot, sV, lv)
                        ucv_all[c] = lv
                    defer(pp_pool, pslot, cur_slice, evac)
                    return None
                return go
            for m in range(MDI):
                for tg in range(TAPS):
                    for gg in range(2):
                        units.append(mk_cv(m, tg, gg))
            return units

        def unit_slice(i):
            if i < 4:
                return 0
            if i < 20:
                return 1 + (i - 4) * 3 // 16
            return 4 + (i - 20) * 12 // 128

        def emit_scan(c, units):
            ui = [0]

            def run_units(n):
                while ui[0] < len(units) and unit_slice(ui[0]) <= n:
                    units[ui[0]](n)
                    ui[0] += 1

            NJ = N * MDI
            rep_n, rslots = {}, {}
            b_ev, p_ev, sc_ev = {}, {}, {}
            eps_ev = {}

            def Bbuf(j):
                return A_B + (j % 2) * Tc

            def SSbuf(j):
                return A_SS + (j % 2) * Tc

            def Pbuf(j):
                return A_P + (j % 2) * Tc

            def emit_rep(n):
                rslot, rtok = rep_pool.acquire()
                rslots[n] = rslot
                first = [True]
                rl = None
                for half in range(2):
                    for hl in range(2):
                        w0 = []
                        if first[0]:
                            w0 = [rtok]
                            if n == 0:
                                w0 += [(sS, bc_ev[c]), (sV, bc_ev_l[c])]
                            first[0] = False
                        srow = A_SBC + hl * Tc + half * 128
                        st = (half == 0 and hl == 0)
                        sp = (half == 1 and hl == 1)
                        rl = op("tensor",
                                lambda e, rslot=rslot, half=half,
                                srow=srow, st=st, sp=sp: e.matmul(
                                    bank(REPB[rslot], half * Tc, Tc),
                                    ab(A_SEL + n * 64, 128),
                                    ab(srow, Tc),
                                    start=st, stop=sp),
                                waits=w0, inc=(sT if sp else None))
                rep_n[n] = rl
                rep_last[c] = rl
                if debug and c == TAPC and n == 0:
                    rv0 = op("vector", lambda e, rslot=rslot: e.tensor_copy(
                        out=a(A_OUT, 512), in_=bank(REPB[rslot], 0, 512)),
                        waits=[(sT, rl)], inc=sV)
                    tap("rep0", A_OUT, 512, (sV, rv0))

            def emit_b(j):
                n, cc = j // MDI, j % MDI
                w = [(sT, rep_n[n])] if cc == 0 else []
                b_ev[j] = op("vector",
                             lambda e, j=j, cc=cc, rslot=rslots[n]:
                             e.tensor_tensor(
                                 out=a(Bbuf(j), Tc),
                                 in0=a(A_DTU + cc * Tc, Tc),
                                 in1=bank(REPB[rslot], 0, Tc), op=Alu.mult),
                             waits=w, inc=sV)

            def emit_p(j):
                n = j // MDI
                w = []
                if ypend[j % 2] is not None:
                    w.append((sG, ypend[j % 2]))
                p_ev[j] = op("vector",
                             lambda e, j=j, rslot=rslots[n]: e.tensor_tensor(
                                 out=a(Pbuf(j), Tc), in0=a(SSbuf(j), Tc),
                                 in1=bank(REPB[rslot], Tc, Tc), op=Alu.mult),
                             waits=w, inc=sV)
                if j % MDI == MDI - 1:
                    rep_pool.release(rslots[n], sV, p_ev[j])

            def emit_y(j):
                cc = j % MDI
                yv = op("gpsimd", lambda e, j=j, cc=cc: e.tensor_tensor(
                    out=a(A_Y + cc * Tc, Tc), in0=a(Pbuf(j), Tc),
                    in1=a(A_Y + cc * Tc, Tc), op=Alu.add),
                    waits=[(sV, p_ev[j])], inc=sG)
                ypend[j % 2] = yv
                return yv

            emit_rep(0)
            emit_b(0)
            sc_val = None
            for j in range(NJ):
                n, cc = j // MDI, j % MDI
                if cc == 0:
                    flush_pend(n - 1)
                    run_units(n)
                if cc == MDI - 1 and n + 1 < N:
                    emit_rep(n + 1)
                dslot, dtok = da_pool.acquire()
                dw = [dtok]
                if j == 0:
                    dw.append((sV, dtp_ev[c]))
                op("scalar", lambda e, cc=cc, n=n, dslot=dslot: e.activation(
                    out=a(A_DA + dslot * Tc, Tc),
                    in_=a(A_DT + cc * Tc, Tc), func=Act.Exp,
                    scale=a(A_ASB + cc * N + n, 1)), waits=dw, inc=sS)
                eps_ev[j] = op("scalar", lambda e, j=j: e.activation(
                    out=a(Bbuf(j), Tc), in_=a(Bbuf(j), Tc),
                    func=Act.Identity, bias=a(A_EPS, 1)),
                    waits=[(sV, b_ev[j])], inc=sS)
                if j > 0:
                    op("scalar", lambda e, j=j: e.activation(
                        out=a(A_ST + (j - 1) % MDI * N + (j - 1) // MDI, 1),
                        in_=a(SSbuf(j - 1) + Tc - 1, 1), func=Act.Copy),
                        waits=[(sV, sc_ev[j - 1])])
                if j + 1 < NJ:
                    emit_b(j + 1)
                if j > 0:
                    emit_p(j - 1)
                sc_val = op("vector",
                            lambda e, j=j, cc=cc, n=n, dslot=dslot:
                            e.tensor_tensor_scan(
                                out=a(SSbuf(j), Tc),
                                data0=a(A_DA + dslot * Tc, Tc),
                                data1=a(Bbuf(j), Tc),
                                initial=a(A_ST + cc * N + n, 1),
                                op0=Alu.mult, op1=Alu.add),
                            waits=[(sS, eps_ev[j])], inc=sV)
                sc_ev[j] = sc_val
                da_pool.release(dslot, sV, sc_val)
                if j > 0:
                    emit_y(j - 1)
            op("scalar", lambda e: e.activation(
                out=a(A_ST + (NJ - 1) % MDI * N + (NJ - 1) // MDI, 1),
                in_=a(SSbuf(NJ - 1) + Tc - 1, 1), func=Act.Copy),
                waits=[(sV, sc_ev[NJ - 1])])
            emit_p(NJ - 1)
            y_last[c] = emit_y(NJ - 1)
            while ui[0] < len(units):
                units[ui[0]](N)
                ui[0] += 1
            flush_pend(99)
            return y_last[c]

        def emit_tail(c):
            ym = None
            for cc in range(MDI):
                ym = op("vector", lambda e, cc=cc: e.scalar_tensor_tensor(
                    out=a(A_Y + cc * Tc, Tc), in0=a(A_Y + cc * Tc, Tc),
                    scalar=1.0, in1=a(A_Z + cc * Tc, Tc),
                    op0=Alu.mult, op1=Alu.mult),
                    waits=([(sG, y_last[c])] if cc == 0 else []),
                    inc=(sV if cc == MDI - 1 else None))
            ym_ev[c] = ym
            if c == TAPC:
                ymtap[0] = tap("ym", A_Y, MDI * Tc, (sV, ym_ev[c]))
            ysp = None
            for cc in range(MDI):
                op("vector", lambda e, cc=cc: e.tensor_copy(
                    out=yh(cc), in_=a(A_Y + cc * Tc, Tc)),
                    waits=([(sV, ym)] if cc == 0 else []))
                ysp = op("vector", lambda e, cc=cc: e.tensor_sub(
                    out=yl(cc), in0=a(A_Y + cc * Tc, Tc), in1=yh(cc)),
                    inc=(sV if cc == MDI - 1 else None))

            outev = {}
            for m in range(MDM):
                pslot, ptok = pp_pool.acquire()
                lastv = None
                for gg in range(2):
                    gi, slot, dv = wload(owT, (m * MDI + gg * 8) * 2)
                    first = [True]
                    for xi, wj in ((0, 0), (1, 0), (0, 1)):
                        for k8 in range(8):
                            k = gg * 8 + k8
                            w0 = []
                            if first[0]:
                                w0 = [dv]
                                if gg == 0:
                                    w0.append(ptok)
                                    w0.append((sV, ysp))
                                first[0] = False
                            xs = yh(k) if xi == 0 else yl(k)
                            st = (gg == 0 and xi == 0 and wj == 0 and k8 == 0)
                            sp = (gg == 1 and xi == 0 and wj == 1 and k8 == 7)
                            lastv = op("tensor",
                                       lambda e, k8=k8, wj=wj, xs=xs,
                                       slot=slot, pslot=pslot, st=st, sp=sp:
                                       e.matmul(
                                           bank(PPB[pslot], 0, Tc),
                                           wtile(slot, k8 * 2 + wj), xs,
                                           start=st, stop=sp),
                                       waits=w0,
                                       inc=(sT if (xi == 0 and wj == 1
                                                   and k8 == 7) else None))
                    ring_last[gi] = lastv
                out_mm_last[c] = lastv
                ew = [(sT, lastv)]
                if m == 0:
                    ew.append((sV, ym_ev[c]))
                    if (c + 1) in tr_last:
                        ew.append((sT, tr_last[c + 1]))
                outev[m] = op("scalar", lambda e, m=m, pslot=pslot:
                              e.activation(
                                  out=a(A_AT + m * Tc, Tc),
                                  in_=bank(PPB[pslot], 0, Tc),
                                  func=Act.Copy), waits=ew, inc=sS)
                pp_pool.release(pslot, sS, outev[m])
            at_tap = (tap("at", A_AT, MDM * Tc, (sS, outev[MDM - 1]))
                      if c == TAPC else None)

            relu_all = None
            for m in range(MFF):
                if m % 2 == 0:
                    gi, slot, dv = wload(f1T, m * KDM)
                pslot, ptok = pp_pool.acquire()
                lastv = None
                first = [True]
                for xi in range(2):
                    for k in range(KDM):
                        w0 = []
                        if first[0]:
                            w0 = [dv, ptok]
                            first[0] = False
                        xs = xnfh(c, k) if xi == 0 else xnfl(c, k)
                        st = (xi == 0 and k == 0)
                        sp = (xi == 1 and k == KDM - 1)
                        lastv = op("tensor",
                                   lambda e, m=m, k=k, xs=xs, slot=slot,
                                   pslot=pslot, st=st, sp=sp: e.matmul(
                                       bank(PPB[pslot], 0, Tc),
                                       wtile(slot, (m % 2) * KDM + k), xs,
                                       start=st, stop=sp),
                                   waits=w0, inc=(sT if sp else None))
                if m % 2 == 1:
                    ring_last[gi] = lastv
                ffn1_last[c] = lastv
                hslot, htok = hf_pool.acquire()
                ew = [(sT, lastv), htok]
                if m == 0:
                    ew.append((sV, ym_ev[c]))
                    if debug and c == TAPC and dttap[0] is not None:
                        ew.append((sTAP, dttap[0]))
                hf = op("scalar", lambda e, m=m, pslot=pslot, hslot=hslot:
                        e.activation(
                            out=a(HFC[hslot], Tc),
                            in_=bank(PPB[pslot], 0, Tc),
                            func=Act.Relu, bias=a(A_B1 + m, 1)),
                        waits=ew, inc=sS)
                pp_pool.release(pslot, sS, hf)
                hcv = op("scalar", lambda e, m=m, hslot=hslot: e.activation(
                    out=hh(m), in_=a(HFC[hslot], Tc), func=Act.Copy), inc=sS)
                relu_all = op("vector", lambda e, m=m, hslot=hslot:
                              e.tensor_sub(
                                  out=hl(m), in0=a(HFC[hslot], Tc),
                                  in1=hh(m)), waits=[(sS, hcv)], inc=sV)
                hf_pool.release(hslot, sV, relu_all)
            relu_ev[c] = relu_all

            fe = {}
            for m in range(MDM):
                pslot, ptok = pp_pool.acquire()
                lastv = None
                for gg in range(2):
                    gi, slot, dv = wload(f2T, m * MFF + gg * G)
                    first = [True]
                    for xi in range(2):
                        for kk in range(G):
                            k = gg * G + kk
                            w0 = []
                            if first[0]:
                                w0 = [dv]
                                if gg == 0:
                                    w0.append(ptok)
                                    if m == 0:
                                        w0.append((sV, relu_all))
                                first[0] = False
                            xs = hh(k) if xi == 0 else hl(k)
                            st = (gg == 0 and xi == 0 and kk == 0)
                            sp = (gg == 1 and xi == 1 and kk == G - 1)
                            lastv = op("tensor",
                                       lambda e, kk=kk, xs=xs, slot=slot,
                                       pslot=pslot, st=st, sp=sp: e.matmul(
                                           bank(PPB[pslot], 0, Tc),
                                           wtile(slot, kk), xs,
                                           start=st, stop=sp),
                                       waits=w0,
                                       inc=(sT if (xi == 1 and kk == G - 1)
                                            else None))
                    ring_last[gi] = lastv
                ffn2_last[c] = lastv
                vw = [(sT, lastv), (sS, outev[m])]
                if c == TAPC and m == 0 and at_tap is not None:
                    vw.append((sTAP, at_tap))
                fe[m] = op("vector",
                           lambda e, m=m, pslot=pslot: e.scalar_tensor_tensor(
                               out=a(A_AT + m * Tc, Tc),
                               in0=bank(PPB[pslot], 0, Tc),
                               scalar=a(A_B2 + m, 1), in1=a(A_AT + m * Tc, Tc),
                               op0=Alu.add, op1=Alu.add), waits=vw, inc=sV)
                pp_pool.release(pslot, sV, fe[m])
            if c == TAPC:
                tap("sf", A_AT, MDM * Tc, (sV, fe[MDM - 1]))

            sx = A_SX + (c % 2) * 2 * DM
            for bt in range(4):
                tt, dq = bt // 2, bt % 2
                slot, tok = tp_pool.acquire()
                last = None
                for j in range(4):
                    dc = dq * 4 + j
                    w0 = []
                    if j == 0:
                        w0 = [tok, (sV, fe[dq * 4 + 3])]
                    last = op("tensor",
                              lambda e, tt=tt, dc=dc, j=j, slot=slot: e.matmul(
                                  bank(TPB[slot], j * 128, 128),
                                  a(A_AT + dc * Tc + tt * 128, 128),
                                  a(A_ID, 128),
                                  start=(j == 0), stop=(j == 3),
                                  is_transpose=True),
                              waits=w0, inc=(sT if j == 3 else None))
                ft_last[c] = last
                vw = [(sT, last)]
                if bt == 0 and c > 0:
                    vw.append(out_store_tok[c - 1])
                fev_last[c] = op("vector",
                                 lambda e, tt=tt, dq=dq, slot=slot,
                                 sx=sx: e.tensor_add(
                                     out=a(A_OUT + tt * DM + dq * 512, 512),
                                     in0=bank(TPB[slot], 0, 512),
                                     in1=a(sx + tt * DM + dq * 512, 512)),
                                 waits=vw, inc=sV)
                tp_pool.release(slot, sV, fev_last[c])

            out_store_tok[c] = dstore(lambda e, c=c: e.dma_start(
                out=out_d[c * Tc:(c + 1) * Tc, :].rearrange(
                    "(t p) d -> p t d", p=128),
                in_=a(A_OUT, 2 * DM).rearrange("p (t d) -> p t d", d=DM)),
                waits=[(sV, fev_last[c])])

        def emit_front_b(c, prolog=False):
            zev = None
            for m in range(MDI, MIP):
                gi, slot, dv = wload(ipT, m * 2 * KDM)
                pslot, ptok = pp_pool.acquire()
                lastv = None
                first = [True]
                for xi, wj in ((0, 0), (1, 0), (0, 1)):
                    for k in range(KDM):
                        w0 = []
                        if first[0]:
                            w0 = [dv, ptok]
                            if m == MDI and prolog:
                                w0.append((sS, xnf_ready[c]))
                            first[0] = False
                        xs = xnfh(c, k) if xi == 0 else xnfl(c, k)
                        st = (xi == 0 and wj == 0 and k == 0)
                        sp = (xi == 0 and wj == 1 and k == KDM - 1)
                        lastv = op("tensor",
                                   lambda e, k=k, wj=wj, xs=xs, slot=slot,
                                   pslot=pslot, st=st, sp=sp: e.matmul(
                                       bank(PPB[pslot], 0, Tc),
                                       wtile(slot, k * 2 + wj), xs,
                                       start=st, stop=sp),
                                   waits=w0, inc=(sT if sp else None))
                ring_last[gi] = lastv
                ew = [(sT, lastv)]
                if m == MDI and not prolog:
                    ew.append((sT, ffn2_last[c - 1]))
                zev = op("scalar", lambda e, m=m, pslot=pslot: e.activation(
                    out=a(A_Z + (m - MDI) * Tc, Tc),
                    in_=bank(PPB[pslot], 0, Tc), func=Act.Silu),
                    waits=ew, inc=sS)
                pp_pool.release(pslot, sS, zev)
            if c == TAPC:
                tap("z", A_Z, MDI * Tc, (sS, zev))

            dt_ev = None
            for m in range(MDI):
                pslot, ptok = pp_pool.acquire()
                lastv = None
                for gg in range(2):
                    gi, slot, dv = wload(dtT, (m * MDI + gg * 8) * 2)
                    first = [True]
                    for xi, wj in ((0, 0), (1, 0), (0, 1)):
                        for k8 in range(8):
                            k = gg * 8 + k8
                            w0 = []
                            if first[0]:
                                w0 = [dv]
                                if gg == 0:
                                    w0.append(ptok)
                                    if m == 0:
                                        w0.append((sV, ucv_all[c]))
                                first[0] = False
                            xs = uh(k) if xi == 0 else ul(k)
                            st = (gg == 0 and xi == 0 and wj == 0
                                  and k8 == 0)
                            sp = (gg == 1 and xi == 0 and wj == 1
                                  and k8 == 7)
                            lastv = op("tensor",
                                       lambda e, k8=k8, wj=wj, xs=xs,
                                       slot=slot, pslot=pslot, st=st, sp=sp:
                                       e.matmul(
                                           bank(PPB[pslot], 0, Tc),
                                           wtile(slot, k8 * 2 + wj), xs,
                                           start=st, stop=sp),
                                       waits=w0,
                                       inc=(sT if (xi == 0 and wj == 1
                                                   and k8 == 7) else None))
                    ring_last[gi] = lastv
                ew = [(sT, lastv)]
                if m == 0 and not prolog:
                    ew.append((sT, ffn2_last[c - 1]))
                dt_ev = op("scalar", lambda e, m=m, pslot=pslot: e.activation(
                    out=a(A_DT + m * Tc, Tc), in_=bank(PPB[pslot], 0, Tc),
                    func=Act.Sigmoid, bias=a(A_DB + m, 1)),
                    waits=ew, inc=sS)
                pp_pool.release(pslot, sS, dt_ev)
            dt_all[c] = dt_ev

            pslot, ptok = pp_pool.acquire()
            lastv = None
            first = [True]
            for half in range(2):
                for xi, wj in ((0, 0), (1, 0), (0, 1)):
                    for k in range(MDI):
                        w0 = []
                        if first[0]:
                            w0 = [ptok]
                            first[0] = False
                        wt = ab(A_BCW + ((half * 2 + wj) * MDI + k) * 8, N)
                        xs = uh(k) if xi == 0 else ul(k)
                        st = (half == 0 and xi == 0 and wj == 0 and k == 0)
                        sp = (half == 1 and xi == 0 and wj == 1
                              and k == MDI - 1)
                        lastv = op("tensor",
                                   lambda e, wt=wt, xs=xs, half=half,
                                   pslot=pslot, st=st, sp=sp: e.matmul(
                                       bank(PPB[pslot], half * Tc, Tc, pn=N),
                                       wt, xs, start=st, stop=sp),
                                   waits=w0, inc=(sT if sp else None))
            ew = [(sT, lastv)]
            if not prolog:
                ew.append((sT, rep_last[c - 1]))
            bc_ev[c] = op("scalar", lambda e, pslot=pslot: e.activation(
                out=ab(A_SBC, 2 * Tc, pn=N),
                in_=bank(PPB[pslot], 0, 2 * Tc, pn=N),
                func=Act.Copy), waits=ew, inc=sS)
            bc_ev_l[c] = op("vector", lambda e, pslot=pslot: e.tensor_sub(
                out=ab(A_SBC + Tc, 2 * Tc, pn=N),
                in0=bank(PPB[pslot], 0, 2 * Tc, pn=N),
                in1=ab(A_SBC, 2 * Tc, pn=N)),
                waits=[(sS, bc_ev[c])], inc=sV)
            pp_pool.release(pslot, sV, bc_ev_l[c])
            if c == TAPC:
                tap("sbch", A_SBC, Tc, (sV, bc_ev_l[c]), pn=N, bf=True)
                tap("sbcl", A_SBC + Tc, Tc, (sV, bc_ev_l[c]), pn=N, bf=True)

            clw = [(sS, dt_all[c])]
            if not prolog:
                clw.append((sT, out_mm_last[c - 1]))
            cl = op("vector", lambda e: e.tensor_scalar(
                out=a(A_DT, MDI * Tc), in0=a(A_DT, MDI * Tc), scalar1=DT_MIN,
                scalar2=1.0, op0=Alu.max, op1=Alu.min), waits=clw, inc=sV)
            dtp = None
            for cc in range(MDI):
                op("vector", lambda e, cc=cc: e.tensor_tensor(
                    out=a(A_B, Tc), in0=a(A_DT + cc * Tc, Tc), in1=uh(cc),
                    op=Alu.mult), waits=([(sV, cl)] if cc == 0 else []))
                op("vector", lambda e, cc=cc: e.tensor_tensor(
                    out=a(A_SS, Tc), in0=a(A_DT + cc * Tc, Tc), in1=ul(cc),
                    op=Alu.mult))
                op("vector", lambda e, cc=cc: e.tensor_add(
                    out=a(A_DTU + cc * Tc, Tc), in0=a(A_B, Tc),
                    in1=a(A_SS, Tc)))
                op("vector", lambda e, cc=cc: e.tensor_scalar(
                    out=a(A_Y + cc * Tc, Tc), in0=uh(cc),
                    scalar1=a(A_DV + cc, 1), scalar2=None, op0=Alu.mult))
                dtp = op("vector", lambda e, cc=cc: e.scalar_tensor_tensor(
                    out=a(A_Y + cc * Tc, Tc), in0=ul(cc),
                    scalar=a(A_DV + cc, 1), in1=a(A_Y + cc * Tc, Tc),
                    op0=Alu.mult, op1=Alu.add),
                    inc=(sV if cc == MDI - 1 else None))
            dtp_ev[c] = dtp
            if c == TAPC:
                dttap[0] = tap("dt", A_DT, MDI * Tc, (sV, dtp_ev[c]))
                tap("dtu", A_DTU, MDI * Tc, (sV, dtp_ev[c]))

        emit_xload(0, extra=prep_store_toks)
        if NCH > 1:
            emit_xload(1)
        emit_ln(0)
        for u in build_units(0):
            u(0)
        flush_pend(99)
        if debug and TAPC == 0:
            utap[0] = tap("u", A_UH, MDI * 128, (sV, ucv_all[0]), bf=True)
            tap("ulo", A_UL, MDI * 128, (sV, ucv_all[0]), bf=True)
            tap("xch", A_XC, MDI * XCW, (sV, xc_all[0]), bf=True)
            tap("xcl", A_XC + MDI * XCW, MDI * XCW, (sV, xc_all[0]), bf=True)
        emit_front_b(0, prolog=True)

        for c in range(NCH):
            if c + 1 < NCH:
                emit_ln(c + 1)
                units = build_units(c + 1)
            else:
                units = []
            sl = emit_scan(c, units)
            if c == TAPC:
                tap("y", A_Y, MDI * Tc, (sG, sl))
            emit_tail(c)
            if c + 1 < NCH:
                emit_front_b(c + 1)
                if debug and c + 1 == TAPC:
                    utap[0] = tap("u", A_UH, MDI * 128, (sV, ucv_all[c + 1]),
                                  bf=True)
                    tap("ulo", A_UL, MDI * 128, (sV, ucv_all[c + 1]), bf=True)
            if c + 2 < NCH:
                emit_xload(c + 2, extra=[(sV, fev_last[c])])

        op("gpsimd", None, waits=dall())

        def replay(eng):
            items = sch.ops[eng]

            def body(e, items=items):
                seen = {}
                for fn, waits, inc, inc_by in items:
                    for s, v in waits:
                        if seen.get(id(s), 0) < v:
                            e.wait_ge(s, v)
                            seen[id(s)] = v
                    if fn is None:
                        continue
                    ins = fn(e)
                    if inc is not None:
                        ins.then_inc(inc, inc_by)
            return body

        block.sync(replay("sync"))
        block.tensor(replay("tensor"))
        block.vector(replay("vector"))
        block.scalar(replay("scalar"))
        block.gpsimd(replay("gpsimd"))

    return nc


_NC_CACHE = {}


def kernel(L0_attn_x, L0_attn_in_proj_w, L0_attn_conv_w, L0_attn_conv_b,
           L0_attn_dt_w, L0_attn_dt_b, L0_attn_A_log, L0_attn_D, L0_attn_B_w,
           L0_attn_C_w, L0_attn_out_w, L0_ffn_W1, L0_ffn_b1, L0_ffn_W2,
           L0_ffn_b2):
    S = L0_attn_x.shape[1]
    if S not in _NC_CACHE:
        _NC_CACHE[S] = build_program(S)
    nc = _NC_CACHE[S]
    w = {
        "in_proj_w": np.ascontiguousarray(L0_attn_in_proj_w, np.float32),
        "conv_w": np.ascontiguousarray(L0_attn_conv_w, np.float32),
        "conv_b": np.ascontiguousarray(L0_attn_conv_b, np.float32),
        "dt_w": np.ascontiguousarray(L0_attn_dt_w, np.float32),
        "dt_b": np.ascontiguousarray(L0_attn_dt_b, np.float32),
        "A_log": np.ascontiguousarray(L0_attn_A_log, np.float32),
        "Dv": np.ascontiguousarray(L0_attn_D, np.float32),
        "B_w": np.ascontiguousarray(L0_attn_B_w, np.float32),
        "C_w": np.ascontiguousarray(L0_attn_C_w, np.float32),
        "out_w": np.ascontiguousarray(L0_attn_out_w, np.float32),
        "W1": np.ascontiguousarray(L0_ffn_W1, np.float32),
        "b1": np.ascontiguousarray(L0_ffn_b1, np.float32),
        "W2": np.ascontiguousarray(L0_ffn_W2, np.float32),
        "b2": np.ascontiguousarray(L0_ffn_b2, np.float32),
    }
    nb = L0_attn_x.shape[0]
    in_maps = [dict(w, x=np.ascontiguousarray(L0_attn_x[b], np.float32))
               for b in range(nb)]
    res = run_bass_kernel_spmd(nc, in_maps, core_ids=list(range(nb)))
    return np.stack([r["out"] for r in res.results], axis=0)


# version 09
# speedup vs baseline: 1.0136x; 1.0136x over previous
import numpy as np
from contextlib import ExitStack

import concourse.bass as bass
import concourse.mybir as mybir
from concourse.bass_utils import run_bass_kernel_spmd

F32 = mybir.dt.float32
BF16 = mybir.dt.bfloat16
Alu = mybir.AluOpType
Act = mybir.ActivationFunctionType
AxX = mybir.AxisListType.X

DM, DI, N, FF, TAPS = 1024, 2048, 16, 4096, 4
Tc = 256
KDM = DM // 128
MDI = DI // 128
MIP = 2 * DI // 128
MFF = FF // 128
MDM = DM // 128
G = 16

TAPC = 0
LN_EPS = 1e-5
FFN_EPS = 1e-5
SCAN_EPS = 1e-6
DT_MIN = 1e-4


class Pool:

    def __init__(self, n):
        self.tok = [None] * n
        self.i = 0
        self.n = n

    def acquire(self):
        s = self.i % self.n
        self.i += 1
        t = self.tok[s]
        self.tok[s] = None
        return s, t

    def release(self, s, sem, val):
        self.tok[s] = (sem, val)


class Sched:
    def __init__(self):
        self.ops = {e: [] for e in ("tensor", "vector", "scalar", "sync", "gpsimd")}
        self.cnt = {}

    def op(self, eng, fn, waits=(), inc=None, inc_by=1):
        w = []
        for t in waits:
            if t is None:
                continue
            s, v = t
            assert v <= self.cnt.get(id(s), 0), f"wait on future event ({eng})"
            if v > 0:
                w.append((s, v))
        val = None
        if inc is not None:
            self.cnt[id(inc)] = self.cnt.get(id(inc), 0) + inc_by
            val = self.cnt[id(inc)]
        self.ops[eng].append((fn, w, inc, inc_by))
        return val


def build_program(S=4096, debug=False):
    NCH = S // Tc
    nc = bass.Bass("TRN2", target_bir_lowering=False, debug=False)
    dbg = {}

    x_d = nc.dram_tensor("x", [S, DM], F32, kind="ExternalInput").ap()
    w_ip = nc.dram_tensor("in_proj_w", [2 * DI, DM], F32, kind="ExternalInput").ap()
    w_cv = nc.dram_tensor("conv_w", [DI, DI, TAPS], F32, kind="ExternalInput").ap()
    b_cv = nc.dram_tensor("conv_b", [DI], F32, kind="ExternalInput").ap()
    w_dt = nc.dram_tensor("dt_w", [DI, DI], F32, kind="ExternalInput").ap()
    b_dt = nc.dram_tensor("dt_b", [DI], F32, kind="ExternalInput").ap()
    a_log = nc.dram_tensor("A_log", [DI, N], F32, kind="ExternalInput").ap()
    d_v = nc.dram_tensor("Dv", [DI], F32, kind="ExternalInput").ap()
    w_b = nc.dram_tensor("B_w", [N, DI], F32, kind="ExternalInput").ap()
    w_c = nc.dram_tensor("C_w", [N, DI], F32, kind="ExternalInput").ap()
    w_ow = nc.dram_tensor("out_w", [DM, DI], F32, kind="ExternalInput").ap()
    w_f1 = nc.dram_tensor("W1", [FF, DM], F32, kind="ExternalInput").ap()
    b_f1 = nc.dram_tensor("b1", [FF], F32, kind="ExternalInput").ap()
    w_f2 = nc.dram_tensor("W2", [DM, FF], F32, kind="ExternalInput").ap()
    b_f2 = nc.dram_tensor("b2", [DM], F32, kind="ExternalInput").ap()
    out_d = nc.dram_tensor("out", [S, DM], F32, kind="ExternalOutput").ap()

    ipT = nc.dram_tensor("ipT", [128, 2 * MIP * KDM * 128], BF16).ap()
    cvT = nc.dram_tensor("cvT", [128, 2 * MDI * TAPS * MDI * 128], BF16).ap()
    dtT = nc.dram_tensor("dtT", [128, 2 * MDI * MDI * 128], BF16).ap()
    owT = nc.dram_tensor("owT", [128, 2 * MDM * MDI * 128], BF16).ap()
    f1T = nc.dram_tensor("f1T", [128, MFF * KDM * 128], BF16).ap()
    f2T = nc.dram_tensor("f2T", [128, MDM * MFF * 128], BF16).ap()

    pos = [0]

    def seg(w):
        c = pos[0]
        pos[0] += w
        return c

    A_SX = seg(2 * 2 * DM)
    A_XNT = seg(2 * DM)
    A_OUT = seg(2 * DM)
    A_AT = seg(2 * DM)
    A_XNF = seg(2 * 2 * KDM * 128)
    XCW = 130
    A_XC = seg(2 * MDI * XCW)
    A_Z = seg(MDI * Tc)
    A_UH = seg(MDI * 128)
    A_UL = seg(MDI * 128)
    A_DT = seg(MDI * Tc)
    A_DTU = seg(MDI * Tc)
    A_Y = seg(MDI * Tc)
    A_WR = seg(2 * G * 64)
    A_DA = seg(2 * Tc)
    A_B = seg(2 * Tc)
    A_SS = seg(2 * Tc)
    A_P = seg(2 * Tc)
    A_ST = seg(Tc)
    A_EPS = seg(1)
    A_ASB = seg(Tc)
    A_BCW = seg(2 * 2 * MDI * 8)
    A_SBC = seg(2 * Tc)
    A_SEL = seg(N * 64)
    A_CB = seg(MDI)
    A_DB = seg(MDI)
    A_DV = seg(MDI)
    A_B1 = seg(MFF)
    A_B2 = seg(MDM)
    A_ID = seg(128)
    A_IDB = seg(64)
    A_ONE = seg(1)
    A_ONR = seg(128)
    A_STAT = seg(16)
    A_GAM = seg(8)
    ACOLS = ((pos[0] + 127) // 128) * 128
    assert ACOLS <= 49152, f"arena too big: {ACOLS}"

    P_NAT = A_Z
    assert A_DTU - A_Z >= 8192 - 4096
    assert A_Y - A_Z >= 8192, "P_NAT needs 8192 contiguous cols"
    P_NH = A_DTU
    sch = Sched()
    P_NHB = A_DTU
    P_NLB = A_Y

    with ExitStack() as est:
        en = est.enter_context
        ar_h = en(nc.sbuf_tensor("arena", [128, ACOLS], F32))
        ps_h = en(nc.psum_tensor("ps", [128, 4096], F32))
        sD = [en(nc.semaphore(f"sD{i}")) for i in range(6)]
        sW = [en(nc.semaphore(f"sW{i}")) for i in range(2)]
        sT = en(nc.semaphore("sT"))
        sS = en(nc.semaphore("sS"))
        sV = en(nc.semaphore("sV"))
        sG = en(nc.semaphore("sG"))
        sTAP = en(nc.semaphore("sTAP"))
        block = en(nc.Block())

        def a(c0, w, p0=0, pn=128):
            return ar_h[p0:p0 + pn, c0:c0 + w]

        def ab(c0, wbf, p0=0, pn=128):
            return ar_h[p0:p0 + pn, c0:c0 + wbf // 2].bitcast(BF16)

        def bank(k, c0=0, w=512, p0=0, pn=128):
            return ps_h[p0:p0 + pn, k * 512 + c0:k * 512 + c0 + w]

        def bankb(k, c0bf=0, wbf=1024, p0=0, pn=128):
            return ps_h[p0:p0 + pn,
                        k * 512 + c0bf // 2:
                        k * 512 + (c0bf + wbf) // 2].bitcast(BF16)

        TPB, PPB, REPB = (0, 1), (4, 5, 6, 7), (2, 3)
        tp_pool, pp_pool, rep_pool = Pool(2), Pool(4), Pool(2)
        da_pool, hf_pool = Pool(2), Pool(2)
        HFC = (A_B, A_SS)

        op = sch.op

        dseq, dcnt = [0], {}
        wseq, wcnt = [0], {}

        def dload(fn, waits=()):
            i = dseq[0] % len(sD)
            dseq[0] += 1
            w = list(waits)
            prev = dcnt.get(i, 0)
            if prev:
                w.append((sD[i], prev))
            v = op("sync", fn, waits=w, inc=sD[i], inc_by=16)
            dcnt[i] = v
            return (sD[i], v)

        def dstore(fn, waits=()):
            i = wseq[0] % len(sW)
            wseq[0] += 1
            w = list(waits)
            prev = wcnt.get(i, 0)
            if prev:
                w.append((sW[i], prev))
            v = op("gpsimd", fn, waits=w, inc=sW[i], inc_by=16)
            wcnt[i] = v
            return (sW[i], v)

        def dall():
            return [(sW[i], wcnt[i]) for i in range(len(sW)) if wcnt.get(i)]

        def tap(name, c0, w, after, pn=128, bf=False):
            if not debug:
                return None
            if bf:
                t = nc.dram_tensor("dbg_" + name, [pn, 2 * w], BF16,
                                   kind="ExternalOutput").ap()
                src = ab(c0, 2 * w, pn=pn)
            else:
                t = nc.dram_tensor("dbg_" + name, [pn, w], F32,
                                   kind="ExternalOutput").ap()
                src = a(c0, w, pn=pn)
            dbg[name] = t
            return op("gpsimd", lambda e, t=t, src=src: e.dma_start(
                out=t, in_=src), waits=[after], inc=sTAP, inc_by=16)

        g_iota = op("gpsimd", lambda e: e.iota(
            out=a(A_ID, 128), pattern=[[-1, 128]], base=0, channel_multiplier=1,
            allow_small_or_imprecise_dtypes=True), inc=sG)
        v_id = op("vector", lambda e: e.tensor_scalar(
            out=a(A_ID, 128), in0=a(A_ID, 128), scalar1=0.0, scalar2=None,
            op0=Alu.is_equal), waits=[(sG, g_iota)], inc=sV)
        v_idb = op("vector", lambda e: e.tensor_copy(
            out=ab(A_IDB, 128), in_=a(A_ID, 128)), waits=[(sV, v_id)], inc=sV)
        g_sel = op("gpsimd", lambda e: e.iota(
            out=a(A_AT, N * 128).rearrange("p (n j) -> p n j", j=128),
            pattern=[[-1, N], [0, 128]], base=0, channel_multiplier=1,
            allow_small_or_imprecise_dtypes=True), inc=sG)
        v_self = op("vector", lambda e: e.tensor_scalar(
            out=ab(A_SEL, N * 128), in0=a(A_AT, N * 128),
            scalar1=0.0, scalar2=None, op0=Alu.is_equal),
            waits=[(sG, g_sel)], inc=sV)
        op("vector", lambda e: e.memset(a(A_SBC, 2 * Tc), 0.0), inc=sV)
        op("vector", lambda e: e.memset(a(A_ONE, 1), 1.0), inc=sV)
        v_onr = op("vector", lambda e: e.memset(a(A_ONR, 128, pn=1), 1.0), inc=sV)
        op("vector", lambda e: e.memset(a(A_XC, 2 * MDI * XCW), 0.0), inc=sV)
        op("vector", lambda e: e.memset(a(A_ST, Tc), 0.0), inc=sV)
        op("vector", lambda e: e.memset(a(A_EPS, 1), SCAN_EPS), inc=sV)

        btoks = []
        for src, dst, wd in ((b_cv, A_CB, MDI), (b_dt, A_DB, MDI),
                             (d_v, A_DV, MDI), (b_f1, A_B1, MFF),
                             (b_f2, A_B2, MDM)):
            btoks.append(dload(lambda e, src=src, dst=dst, wd=wd: e.dma_start(
                out=a(dst, wd), in_=src.rearrange("(c p) -> p c", p=128),
                allow_slow_non_contiguous=True)))
        d_alog = dload(lambda e: e.dma_start(
            out=a(A_ASB, Tc).rearrange("p (c n) -> p c n", n=N),
            in_=a_log.rearrange("(c p) n -> p c n", p=128)))
        s_exp = op("scalar", lambda e: e.activation(
            out=a(A_ASB, Tc), in_=a(A_ASB, Tc), func=Act.Exp),
            waits=[d_alog] + btoks, inc=sS)
        v_asb = op("vector", lambda e: e.tensor_scalar(
            out=a(A_ASB, Tc), in0=a(A_ASB, Tc), scalar1=-1.0, scalar2=None,
            op0=Alu.mult), waits=[(sS, s_exp)] + btoks, inc=sV)
        tap("asb", A_ASB, Tc, (sV, v_asb))

        nat_tok = [None]
        str_tok = [None]
        n_prep_stores = [0]

        last = None
        for wi, wsrc in enumerate((w_b, w_c)):
            d0 = dload(lambda e, w=wsrc: e.dma_start(
                out=a(P_NAT, DI, pn=N), in_=w), waits=[nat_tok[0]])
            sh = op("scalar", lambda e: e.activation(
                out=ab(P_NHB, DI, pn=N), in_=a(P_NAT, DI, pn=N),
                func=Act.Copy), waits=[d0], inc=sS)
            sl = op("vector", lambda e: e.tensor_sub(
                out=ab(P_NLB, DI, pn=N), in0=a(P_NAT, DI, pn=N),
                in1=ab(P_NHB, DI, pn=N)), waits=[(sS, sh)], inc=sV)
            for hl, srcb in ((0, P_NHB), (1, P_NLB)):
                for half in range(2):
                    slot, tok = tp_pool.acquire()
                    for j in range(8):
                        k = half * 8 + j
                        last = op("tensor",
                                  lambda e, k=k, j=j, slot=slot, srcb=srcb:
                                  e.matmul(
                                      bankb(TPB[slot], j * N, N),
                                      ab(srcb + k * 64, 128, pn=N),
                                      ab(A_IDB, N, pn=N),
                                      start=(j == 0), stop=(j == 7),
                                      is_transpose=True),
                                  waits=([(sV, sl), (sV, v_idb), tok]
                                         if j == 0 else []),
                                  inc=(sT if j == 7 else None))
                    ev = op("scalar",
                            lambda e, slot=slot, wi=wi, hl=hl, half=half:
                            e.activation(
                                out=ab(A_BCW +
                                       ((wi * 2 + hl) * MDI + half * 8) * 8,
                                       8 * N),
                                in_=bankb(TPB[slot], 0, 8 * N),
                                func=Act.Copy),
                            waits=[(sT, last)], inc=sS)
                    tp_pool.release(slot, sS, ev)
            nat_tok[0] = (sT, last)

        def prep_weight(nat_src_fn, nblocks, ktiles, scratch, width,
                        quant=None, strided_taps=False):
            for m in range(nblocks):
                d0 = dload(lambda e, m=m: e.dma_start(
                    out=a(P_NAT, width), in_=nat_src_fn(m)),
                    waits=[nat_tok[0]])
                if quant is None:
                    sh = op("scalar", lambda e, width=width: e.activation(
                        out=ab(P_NHB, width), in_=a(P_NAT, width),
                        func=Act.Copy), waits=[d0], inc=sS)
                    sl = op("vector", lambda e, width=width: e.tensor_sub(
                        out=ab(P_NLB, width), in0=a(P_NAT, width),
                        in1=ab(P_NHB, width)), waits=[(sS, sh)], inc=sV)
                    nat_tok[0] = (sV, sl)
                    gate = (sV, sl)
                    ntile2 = 2 * ktiles
                else:
                    vq = quant(m, d0)
                    nat_tok[0] = (sV, vq)
                    gate = (sV, vq)
                    ntile2 = ktiles
                last_ev = None
                last = None
                nb = (ntile2 + 7) // 8
                for bt in range(nb):
                    slot, tok = tp_pool.acquire()
                    nin = min(8, ntile2 - bt * 8)
                    for j in range(nin):
                        t2 = bt * 8 + j
                        if quant is None:
                            k, hl = t2 // 2, t2 % 2
                            srcb = P_NHB if hl == 0 else P_NLB
                        else:
                            k, srcb = t2, P_NLB
                        if strided_taps:
                            tap_, i = k // MDI, k % MDI
                            src = ab(srcb, width).rearrange(
                                "p (i t) -> p i t", t=TAPS)[
                                :, i * 128:(i + 1) * 128, tap_]
                        else:
                            src = ab(srcb + k * 64, 128)
                        w0 = [tok] + ([gate, (sV, v_idb)]
                                      if (bt == 0 and j == 0) else [])
                        last = op("tensor",
                                  lambda e, src=src, slot=slot, j=j, nin=nin:
                                  e.matmul(
                                      bankb(TPB[slot], j * 128, 128), src,
                                      ab(A_IDB, 128), start=(j == 0),
                                      stop=(j == nin - 1), is_transpose=True),
                                  waits=(w0 if j == 0 else []),
                                  inc=(sT if j == nin - 1 else None))
                    ev = op("scalar",
                            lambda e, slot=slot, bt=bt, nin=nin: e.activation(
                                out=ab(P_STR + bt * 512, nin * 128),
                                in_=bankb(TPB[slot], 0, nin * 128),
                                func=Act.Copy),
                            waits=[(sT, last)] + ([str_tok[0]]
                                                  if bt == 0 else []),
                            inc=sS)
                    tp_pool.release(slot, sS, ev)
                    last_ev = ev
                if quant is None:
                    nat_tok[0] = (sT, last)
                else:
                    qtm2_tok[0] = (sT, last)
                str_tok[0] = dstore(
                    lambda e, m=m, nt=ntile2, scr=scratch: e.dma_start(
                        out=scr[:, m * nt * 128:(m + 1) * nt * 128],
                        in_=ab(P_STR, nt * 128)),
                    waits=[(sS, last_ev)])
                n_prep_stores[0] += 1

        P_STR = A_SX

        prep_weight(lambda m: w_ip[m * 128:(m + 1) * 128, :], MIP, KDM, ipT, DM)
        prep_weight(lambda m: w_cv[m * 128:(m + 1) * 128, :, :].rearrange(
            "p i t -> p (i t)"), MDI, TAPS * MDI, cvT, DI * TAPS,
            strided_taps=True)
        prep_weight(lambda m: w_dt[m * 128:(m + 1) * 128, :], MDI, MDI, dtT, DI)
        prep_weight(lambda m: w_ow[m * 128:(m + 1) * 128, :], MDM, MDI, owT, DI)

        def gamma_pass(wsrc, nblocks, width, gcol):
            op("vector", lambda e: e.memset(a(gcol, 1), 0.0), inc=sV)
            rv = None
            for m in range(nblocks):
                d0 = dload(lambda e, m=m, w=wsrc: e.dma_start(
                    out=a(P_NAT, width), in_=w[m * 128:(m + 1) * 128, :]),
                    waits=[nat_tok[0]])
                gv = op("vector", lambda e, width=width: e.reduce_sum(
                    out=a(gcol + 1, 1), in_=a(P_NAT, width), axis=AxX,
                    apply_absolute_value=True), waits=[d0], inc=sV)
                rv = op("vector", lambda e: e.tensor_add(
                    out=a(gcol, 1), in0=a(gcol, 1), in1=a(gcol + 1, 1)),
                    waits=[(sV, gv)], inc=sV)
                nat_tok[0] = (sV, rv)
            mm = op("tensor", lambda e: e.matmul(
                bank(PPB[0], 0, 1, pn=1), a(gcol, 1), a(A_ONE, 1),
                start=True, stop=True), waits=[(sV, rv)], inc=sT)
            e1 = op("scalar", lambda e: e.activation(
                out=a(gcol + 2, 1, pn=1), in_=bank(PPB[0], 0, 1, pn=1),
                func=Act.Copy), waits=[(sT, mm)], inc=sS)
            nelem = float(nblocks * 128 * width)
            w1_ = op("vector", lambda e, nelem=nelem: e.tensor_scalar(
                out=a(gcol + 3, 1, pn=1), in0=a(gcol + 2, 1, pn=1),
                scalar1=1.0 / nelem, scalar2=FFN_EPS, op0=Alu.mult,
                op1=Alu.add), waits=[(sS, e1)], inc=sV)
            w2_ = op("vector", lambda e: e.reciprocal(
                out=a(gcol + 4, 1, pn=1), in_=a(gcol + 3, 1, pn=1)),
                waits=[(sV, w1_)], inc=sV)
            w3_ = op("vector", lambda e: e.tensor_mul(
                out=a(gcol + 5, 1, pn=1), in0=a(gcol + 3, 1, pn=1),
                in1=a(gcol + 4, 1, pn=1)), waits=[(sV, w2_)], inc=sV)
            w4_ = op("vector", lambda e: e.tensor_scalar(
                out=a(gcol + 5, 1, pn=1), in0=a(gcol + 5, 1, pn=1),
                scalar1=-1.0, scalar2=2.0, op0=Alu.mult, op1=Alu.add),
                waits=[(sV, w3_)], inc=sV)
            v2 = op("vector", lambda e: e.tensor_mul(
                out=a(gcol + 4, 1, pn=1), in0=a(gcol + 4, 1, pn=1),
                in1=a(gcol + 5, 1, pn=1)), waits=[(sV, w4_)], inc=sV)
            mm2 = op("tensor", lambda e: e.matmul(
                bank(PPB[1], 0, 1), a(A_ONR, 128, pn=1), a(gcol + 4, 1, pn=1),
                start=True, stop=True), waits=[(sV, v2), (sV, v_onr)], inc=sT)
            e2 = op("scalar", lambda e: e.activation(
                out=a(gcol + 6, 1), in_=bank(PPB[1], 0, 1), func=Act.Copy),
                waits=[(sT, mm2)], inc=sS)
            return gcol + 6, e2

        qtm_tok = [None]
        qtm2_tok = [None]

        def mk_quant(rc, rev, g1c, g2c, width):
            def quant(m, d0):
                op("vector", lambda e: e.tensor_scalar(
                    out=a(g1c, width), in0=a(P_NAT, width), scalar1=a(rc, 1),
                    scalar2=0.5, op0=Alu.mult, op1=Alu.is_gt),
                    waits=[d0, (sS, rev), qtm_tok[0], qtm2_tok[0]], inc=sV)
                g2v = op("vector", lambda e: e.tensor_scalar(
                    out=a(g2c, width), in0=a(P_NAT, width), scalar1=a(rc, 1),
                    scalar2=-0.5, op0=Alu.mult, op1=Alu.is_lt), inc=sV)
                vq = op("vector", lambda e: e.tensor_sub(
                    out=ab(P_NLB, width), in0=a(g1c, width), in1=a(g2c, width)),
                    waits=[(sV, g2v)], inc=sV)
                qtm_tok[0] = (sV, vq)
                return vq
            return quant

        r1c, r1ev = gamma_pass(w_f1, MFF, DM, A_GAM)
        prep_weight(lambda m: w_f1[m * 128:(m + 1) * 128, :], MFF, KDM, f1T, DM,
                    quant=mk_quant(r1c, r1ev, A_AT, A_AT + 1024, DM))
        r2c, r2ev = gamma_pass(w_f2, MDM, FF, A_GAM)
        prep_weight(lambda m: w_f2[m * 128:(m + 1) * 128, :], MDM, MFF, f2T, FF,
                    quant=mk_quant(r2c, r2ev, A_DTU, A_AT, FF))
        v_rz = op("vector", lambda e: e.memset(a(A_XC, 2 * MDI * XCW), 0.0),
                  inc=sV)

        PREP_STORES = n_prep_stores[0]
        prep_store_toks = dall()

        conv_last, ffn1_last, ffn2_last = {}, {}, {}
        out_mm_last, ft_last, fev_last, ym_ev, a_ev = {}, {}, {}, {}, {}
        out_store_tok = {}
        xload, dtp_ev, bc_ev, bc_ev_l, rep_last, xnf_ready = {}, {}, {}, {}, {}, {}
        tr_last, xc_all, dt_all, relu_ev, ucv_all = {}, {}, {}, {}, {}
        y_last, z_ev = {}, {}
        ypend = {0: None, 1: None}
        scpend = {0: None, 1: None}
        ppend = {0: None, 1: None}
        wgrp = [0]
        ring_last = {}
        utap, dttap, ymtap = [None], [None], [None]

        def wload(scratch, t0, ntiles=G, extra=()):
            gi = wgrp[0]
            wgrp[0] += 1
            slot = gi % 2
            waits = list(extra)
            if gi >= 2:
                waits.append((sT, ring_last[gi - 2]))
            dv = dload(lambda e, scr=scratch, t0=t0, nt=ntiles, slot=slot:
                       e.dma_start(
                           out=ab(A_WR + slot * G * 64, nt * 128),
                           in_=scr[:, t0 * 128:(t0 + nt) * 128]),
                       waits=waits)
            return gi, slot, dv

        def wtile(slot, i):
            return ab(A_WR + slot * G * 64 + i * 64, 128)

        def xnfh(c, k):
            return ab(A_XNF + (c % 2) * 2 * KDM * 128 + k * 128, Tc)

        def xnfl(c, k):
            return ab(A_XNF + (c % 2) * 2 * KDM * 128 + KDM * 128 + k * 128,
                      Tc)

        def xch(i, t0, w):
            return ab(A_XC + i * XCW, 2 * XCW)[:, t0:t0 + w]

        def xcl(i, t0, w):
            return ab(A_XC + MDI * XCW + i * XCW, 2 * XCW)[:, t0:t0 + w]

        def uh(k):
            return ab(A_UH + k * 128, Tc)

        def ul(k):
            return ab(A_UL + k * 128, Tc)

        def yh(k):
            return ab(A_DTU + k * 128, Tc)

        def yl(k):
            return ab(A_DTU + 2048 + k * 128, Tc)

        def hh(k):
            return ab(A_Z + k * 128, Tc)

        def hl(k):
            return ab(A_DT + k * 128, Tc)

        def emit_xload(c, extra=()):
            xload[c] = dload(lambda e, c=c: e.dma_start(
                out=a(A_SX + (c % 2) * 2 * DM, 2 * DM).rearrange(
                    "p (t d) -> p t d", d=DM),
                in_=x_d[c * Tc:(c + 1) * Tc, :].rearrange(
                    "(t p) d -> p t d", p=128)),
                waits=list(extra))

        def emit_ln(c):
            sx = A_SX + (c % 2) * 2 * DM
            hw = [xload[c]]
            if c == 0:
                hw.append((sV, v_rz))
            if c > 0:
                hw.append((sT, conv_last[c - 1]))
            if c > 1:
                hw.append((sT, ft_last[c - 2]))
            op("vector", lambda e: e.tensor_copy(
                out=ab(A_XC, 4 * MDI * XCW).rearrange(
                    "p (m w) -> p m w", w=2 * XCW)[:, :, 0:3],
                in_=ab(A_XC, 4 * MDI * XCW).rearrange(
                    "p (m w) -> p m w", w=2 * XCW)[:, :, Tc:Tc + 3]),
                waits=hw, inc=sV)
            vv = None
            for tt in range(2):
                mv = op("vector", lambda e, tt=tt, sx=sx: e.reduce_sum(
                    out=a(A_STAT + 6 + tt, 1), in_=a(sx + tt * DM, DM),
                    axis=AxX), inc=sV)
                nv = op("vector", lambda e, tt=tt: e.tensor_scalar(
                    out=a(A_STAT + 8 + tt, 1), in0=a(A_STAT + 6 + tt, 1),
                    scalar1=-1.0 / DM, scalar2=None, op0=Alu.mult),
                    waits=[(sV, mv)], inc=sV)
                xw = [(sV, nv)]
                if tt == 0 and c > 0:
                    xw.append((sT, tr_last[c - 1]))
                xnv = op("vector", lambda e, tt=tt, sx=sx: e.tensor_scalar(
                    out=a(A_XNT + tt * DM, DM), in0=a(sx + tt * DM, DM),
                    scalar1=a(A_STAT + 8 + tt, 1), scalar2=None, op0=Alu.add),
                    waits=xw, inc=sV)
                sqv = op("vector", lambda e, tt=tt: e.tensor_mul(
                    out=a(A_AT + tt * DM, DM), in0=a(A_XNT + tt * DM, DM),
                    in1=a(A_XNT + tt * DM, DM)), waits=[(sV, xnv)], inc=sV)
                qv = op("vector", lambda e, tt=tt: e.reduce_sum(
                    out=a(A_STAT + 10 + tt, 1), in_=a(A_AT + tt * DM, DM),
                    axis=AxX), waits=[(sV, sqv)], inc=sV)
                vv = op("vector", lambda e, tt=tt: e.tensor_scalar(
                    out=a(A_STAT + tt, 1), in0=a(A_STAT + 10 + tt, 1),
                    scalar1=1.0 / DM, scalar2=LN_EPS, op0=Alu.mult,
                    op1=Alu.add), waits=[(sV, qv)], inc=sV)
            sq = op("scalar", lambda e: e.activation(
                out=a(A_STAT + 2, 2), in_=a(A_STAT, 2), func=Act.Sqrt),
                waits=[(sV, vv)], inc=sS)
            rcv = op("vector", lambda e: e.reciprocal(
                out=a(A_STAT + 4, 2), in_=a(A_STAT + 2, 2)), waits=[(sS, sq)],
                inc=sV)
            nmv = None
            for tt in range(2):
                nmv = op("vector", lambda e, tt=tt: e.tensor_scalar(
                    out=a(A_XNT + tt * DM, DM), in0=a(A_XNT + tt * DM, DM),
                    scalar1=a(A_STAT + 4 + tt, 1), scalar2=None, op0=Alu.mult),
                    waits=([(sV, rcv)] if tt == 0 else []), inc=sV)
            vh = op("vector", lambda e: e.tensor_copy(
                out=ab(A_AT, 2 * DM), in_=a(A_XNT, 2 * DM)),
                waits=[(sV, nmv)], inc=sV)
            vxn = op("vector", lambda e: e.tensor_sub(
                out=ab(A_AT + DM, 2 * DM), in0=a(A_XNT, 2 * DM),
                in1=ab(A_AT, 2 * DM)), waits=[(sV, vh)], inc=sV)
            a_ev[c] = vxn
            if c == TAPC:
                tap("xnt", A_XNT, 2 * DM, (sV, a_ev[c]))
                tap("stat", A_STAT, 16, (sV, rcv))

        pend = {}

        def acq(pool, cur_slice):
            s, t = pool.acquire()
            key = (id(pool), s)
            if t is None and key in pend:
                pend.pop(key)[1]()
                t = pool.tok[s]
                pool.tok[s] = None
            return s, t

        def defer(pool, s, sl, fn):
            pend[(id(pool), s)] = (sl, fn)

        def flush_pend(max_slice):
            for key in list(pend):
                if pend[key][0] <= max_slice:
                    pend.pop(key)[1]()

        def build_units(c):
            units = []

            def mk_tp(bt):
                def go(cur_slice=0):
                    hlv, tt = bt // 2, bt % 2
                    srcb = A_AT + hlv * DM + tt * 512
                    slot, tok = tp_pool.acquire()
                    last = None
                    for j in range(8):
                        last = op("tensor",
                                  lambda e, j=j, slot=slot, srcb=srcb: e.matmul(
                                      bankb(TPB[slot], j * 128, 128),
                                      ab(srcb + j * 64, 128), ab(A_IDB, 128),
                                      start=(j == 0), stop=(j == 7),
                                      is_transpose=True),
                                  waits=([(sV, a_ev[c]), tok] if j == 0
                                         else []),
                                  inc=(sT if j == 7 else None))
                    tr_last[c] = last
                    ew = [(sT, last)]
                    if bt == 0 and c > 1:
                        ew.append((sT, ffn1_last[c - 2]))
                    dstb = (A_XNF + (c % 2) * 2 * KDM * 128 + hlv * KDM * 128)
                    ev = op("scalar",
                            lambda e, slot=slot, dstb=dstb, tt=tt:
                            e.activation(
                                out=ab(dstb, KDM * Tc).rearrange(
                                    "p (dc t) -> p dc t", t=Tc)[
                                    :, :, tt * 128:(tt + 1) * 128],
                                in_=bankb(TPB[slot], 0, 1024).rearrange(
                                    "p (j t) -> p j t", t=128),
                                func=Act.Copy), waits=ew, inc=sS)
                    tp_pool.release(slot, sS, ev)
                    xnf_ready[c] = ev
                    return None
                return go
            for bt in range(4):
                units.append(mk_tp(bt))

            def mk_ip(m):
                def go(cur_slice=0):
                    gi, slot, dv = wload(ipT, m * 2 * KDM)
                    pslot, ptok = acq(pp_pool, cur_slice)
                    lastv = None
                    first = [True]
                    for xi, wj in ((0, 0), (1, 0), (0, 1)):
                        for k in range(KDM):
                            w0 = []
                            if first[0]:
                                w0 = [dv, ptok, (sS, xnf_ready[c])]
                                first[0] = False
                            xs = xnfh(c, k) if xi == 0 else xnfl(c, k)
                            st = (xi == 0 and wj == 0 and k == 0)
                            sp = (xi == 0 and wj == 1 and k == KDM - 1)
                            lastv = op("tensor",
                                       lambda e, k=k, wj=wj, xs=xs, slot=slot,
                                       pslot=pslot, st=st, sp=sp: e.matmul(
                                           bank(PPB[pslot], 0, Tc),
                                           wtile(slot, k * 2 + wj), xs,
                                           start=st, stop=sp),
                                       waits=w0, inc=(sT if sp else None))
                    ring_last[gi] = lastv

                    def evac():
                        ew = [(sT, lastv), (sV, a_ev[c])]
                        ev = op("scalar", lambda e, m=m, pslot=pslot:
                                e.activation(
                                    out=xch(m, 3, Tc),
                                    in_=bank(PPB[pslot], 0, Tc),
                                    func=Act.Copy), waits=ew, inc=sS)
                        lv = op("vector", lambda e, m=m, pslot=pslot:
                                e.tensor_sub(
                                    out=xcl(m, 3, Tc),
                                    in0=bank(PPB[pslot], 0, Tc),
                                    in1=xch(m, 3, Tc)), waits=[(sS, ev)],
                                inc=sV)
                        pp_pool.release(pslot, sV, lv)
                        xc_all[c] = lv
                    defer(pp_pool, pslot, cur_slice, evac)
                    return None
                return go
            for m in range(MDI):
                units.append(mk_ip(m))

            cvst = {}

            def mk_cv(m, tg, gg):
                def go(cur_slice=0):
                    gi, slot, dv = wload(
                        cvT, ((m * TAPS + tg) * MDI + gg * 8) * 2)
                    if tg == 0 and gg == 0:
                        cvst[m] = acq(pp_pool, cur_slice)
                    pslot, ptok = cvst[m]
                    first_mm = (tg == 0 and gg == 0)
                    lastv = None
                    first = [True]
                    for xi, wj in ((0, 0), (1, 0), (0, 1)):
                        for i8 in range(8):
                            i = gg * 8 + i8
                            w0 = []
                            if first[0]:
                                w0 = [dv]
                                if first_mm:
                                    w0.append(ptok)
                                    if m == 0:
                                        w0.append((sV, xc_all[c]))
                                first[0] = False
                            xs = (xch(i, tg, Tc) if xi == 0
                                  else xcl(i, tg, Tc))
                            st = (first_mm and xi == 0 and wj == 0
                                  and i8 == 0)
                            sp = (tg == TAPS - 1 and gg == 1 and xi == 0
                                  and wj == 1 and i8 == 7)
                            lastv = op("tensor",
                                       lambda e, i8=i8, wj=wj, xs=xs,
                                       slot=slot, pslot=pslot, st=st, sp=sp:
                                       e.matmul(
                                           bank(PPB[pslot], 0, Tc),
                                           wtile(slot, i8 * 2 + wj), xs,
                                           start=st, stop=sp),
                                       waits=w0,
                                       inc=(sT if (xi == 0 and wj == 1
                                                   and i8 == 7) else None))
                    ring_last[gi] = lastv
                    if not (tg == TAPS - 1 and gg == 1):
                        return None
                    conv_last[c] = lastv

                    def evac():
                        ew = [(sT, lastv)]
                        if m == 0 and c > 0:
                            ew.append((sV, dtp_ev[c - 1]))
                            if debug and c == TAPC + 1 and utap[0] is not None:
                                ew.append((sTAP, utap[0]))
                        ev = op("scalar", lambda e, m=m, pslot=pslot:
                                e.activation(
                                    out=uh(m), in_=bank(PPB[pslot], 0, Tc),
                                    func=Act.Identity, bias=a(A_CB + m, 1)),
                                waits=ew, inc=sS)
                        lv = op("vector", lambda e, m=m, pslot=pslot:
                                e.scalar_tensor_tensor(
                                    out=ul(m), in0=bank(PPB[pslot], 0, Tc),
                                    scalar=a(A_CB + m, 1), in1=uh(m),
                                    op0=Alu.add, op1=Alu.subtract),
                                waits=[(sS, ev)], inc=sV)
                        pp_pool.release(pslot, sV, lv)
                        ucv_all[c] = lv
                    defer(pp_pool, pslot, cur_slice, evac)
                    return None
                return go
            for m in range(MDI):
                for tg in range(TAPS):
                    for gg in range(2):
                        units.append(mk_cv(m, tg, gg))
            return units

        def unit_slice(i):
            if i < 4:
                return 0
            if i < 20:
                return 1 + (i - 4) * 3 // 16
            return 4 + (i - 20) * 12 // 128

        def emit_scan(c, units):
            ui = [0]

            def run_units(n):
                while ui[0] < len(units) and unit_slice(ui[0]) <= n:
                    units[ui[0]](n)
                    ui[0] += 1

            NJ = N * MDI
            rep_n, rslots = {}, {}
            b_ev, p_ev, sc_ev = {}, {}, {}
            eps_ev = {}

            def Bbuf(j):
                return A_B + (j % 2) * Tc

            def SSbuf(j):
                return A_SS + (j % 2) * Tc

            def Pbuf(j):
                return A_P + (j % 2) * Tc

            def emit_rep(n):
                rslot, rtok = rep_pool.acquire()
                rslots[n] = rslot
                first = [True]
                rl = None
                for half in range(2):
                    for hl in range(2):
                        w0 = []
                        if first[0]:
                            w0 = [rtok]
                            if n == 0:
                                w0 += [(sS, bc_ev[c]), (sV, bc_ev_l[c])]
                            first[0] = False
                        srow = A_SBC + hl * Tc + half * 128
                        st = (half == 0 and hl == 0)
                        sp = (half == 1 and hl == 1)
                        rl = op("tensor",
                                lambda e, rslot=rslot, half=half,
                                srow=srow, st=st, sp=sp: e.matmul(
                                    bank(REPB[rslot], half * Tc, Tc),
                                    ab(A_SEL + n * 64, 128),
                                    ab(srow, Tc),
                                    start=st, stop=sp),
                                waits=w0, inc=(sT if sp else None))
                rep_n[n] = rl
                rep_last[c] = rl
                if debug and c == TAPC and n == 0:
                    rv0 = op("vector", lambda e, rslot=rslot: e.tensor_copy(
                        out=a(A_OUT, 512), in_=bank(REPB[rslot], 0, 512)),
                        waits=[(sT, rl)], inc=sV)
                    tap("rep0", A_OUT, 512, (sV, rv0))

            def emit_b(j):
                n, cc = j // MDI, j % MDI
                w = [(sT, rep_n[n])] if cc == 0 else []
                if scpend[j % 2] is not None:
                    w.append((sV, scpend[j % 2]))
                b_ev[j] = op("vector",
                             lambda e, j=j, cc=cc, rslot=rslots[n]:
                             e.tensor_tensor(
                                 out=a(Bbuf(j), Tc),
                                 in0=a(A_DTU + cc * Tc, Tc),
                                 in1=bank(REPB[rslot], 0, Tc), op=Alu.mult),
                             waits=w, inc=sV)

            def emit_p(j):
                n = j // MDI
                w = [(sV, sc_ev[j])]
                if ypend[j % 2] is not None:
                    w.append((sG, ypend[j % 2]))
                p_ev[j] = op("vector",
                             lambda e, j=j, rslot=rslots[n]: e.tensor_tensor(
                                 out=a(Pbuf(j), Tc), in0=a(SSbuf(j), Tc),
                                 in1=bank(REPB[rslot], Tc, Tc), op=Alu.mult),
                             waits=w, inc=sV)
                ppend[j % 2] = p_ev[j]
                if j % MDI == MDI - 1:
                    rep_pool.release(rslots[n], sV, p_ev[j])

            def emit_y(j):
                cc = j % MDI
                yv = op("gpsimd", lambda e, j=j, cc=cc: e.tensor_tensor(
                    out=a(A_Y + cc * Tc, Tc), in0=a(Pbuf(j), Tc),
                    in1=a(A_Y + cc * Tc, Tc), op=Alu.add),
                    waits=[(sV, p_ev[j])], inc=sG)
                ypend[j % 2] = yv
                return yv

            emit_rep(0)
            emit_b(0)
            sc_val = None
            for j in range(NJ):
                n, cc = j // MDI, j % MDI
                if cc == 0:
                    flush_pend(n - 1)
                    run_units(n)
                if cc == MDI - 1 and n + 1 < N:
                    emit_rep(n + 1)
                dslot, dtok = da_pool.acquire()
                dw = [dtok]
                if j == 0:
                    dw.append((sV, dtp_ev[c]))
                op("scalar", lambda e, cc=cc, n=n, dslot=dslot: e.activation(
                    out=a(A_DA + dslot * Tc, Tc),
                    in_=a(A_DT + cc * Tc, Tc), func=Act.Exp,
                    scale=a(A_ASB + cc * N + n, 1)), waits=dw, inc=sS)
                eps_ev[j] = op("scalar", lambda e, j=j: e.activation(
                    out=a(Bbuf(j), Tc), in_=a(Bbuf(j), Tc),
                    func=Act.Identity, bias=a(A_EPS, 1)),
                    waits=[(sV, b_ev[j])], inc=sS)
                if j > 0:
                    op("scalar", lambda e, j=j: e.activation(
                        out=a(A_ST + (j - 1) % MDI * N + (j - 1) // MDI, 1),
                        in_=a(SSbuf(j - 1) + Tc - 1, 1), func=Act.Copy),
                        waits=[(sV, sc_ev[j - 1])])
                if j + 1 < NJ:
                    emit_b(j + 1)
                if j > 0:
                    emit_p(j - 1)
                scw = [(sS, eps_ev[j])]
                if ppend[j % 2] is not None:
                    scw.append((sV, ppend[j % 2]))
                sc_val = op("vector",
                            lambda e, j=j, cc=cc, n=n, dslot=dslot:
                            e.tensor_tensor_scan(
                                out=a(SSbuf(j), Tc),
                                data0=a(A_DA + dslot * Tc, Tc),
                                data1=a(Bbuf(j), Tc),
                                initial=a(A_ST + cc * N + n, 1),
                                op0=Alu.mult, op1=Alu.add),
                            waits=scw, inc=sV)
                sc_ev[j] = sc_val
                scpend[j % 2] = sc_val
                da_pool.release(dslot, sV, sc_val)
                if j > 0:
                    emit_y(j - 1)
            op("scalar", lambda e: e.activation(
                out=a(A_ST + (NJ - 1) % MDI * N + (NJ - 1) // MDI, 1),
                in_=a(SSbuf(NJ - 1) + Tc - 1, 1), func=Act.Copy),
                waits=[(sV, sc_ev[NJ - 1])])
            emit_p(NJ - 1)
            y_last[c] = emit_y(NJ - 1)
            while ui[0] < len(units):
                units[ui[0]](N)
                ui[0] += 1
            flush_pend(99)
            return y_last[c]

        def emit_tail(c):
            ym = None
            for cc in range(MDI):
                ym = op("vector", lambda e, cc=cc: e.scalar_tensor_tensor(
                    out=a(A_Y + cc * Tc, Tc), in0=a(A_Y + cc * Tc, Tc),
                    scalar=1.0, in1=a(A_Z + cc * Tc, Tc),
                    op0=Alu.mult, op1=Alu.mult),
                    waits=([(sG, y_last[c]), (sS, z_ev[c])] if cc == 0 else []),
                    inc=(sV if cc == MDI - 1 else None))
            ym_ev[c] = ym
            if c == TAPC:
                ymtap[0] = tap("ym", A_Y, MDI * Tc, (sV, ym_ev[c]))
            ysp = None
            for cc in range(MDI):
                yhv = op("vector", lambda e, cc=cc: e.tensor_copy(
                    out=yh(cc), in_=a(A_Y + cc * Tc, Tc)),
                    waits=([(sV, ym)] if cc == 0 else []), inc=sV)
                ysp = op("vector", lambda e, cc=cc: e.tensor_sub(
                    out=yl(cc), in0=a(A_Y + cc * Tc, Tc), in1=yh(cc)),
                    waits=[(sV, yhv)],
                    inc=(sV if cc == MDI - 1 else None))

            outev = {}
            for m in range(MDM):
                pslot, ptok = pp_pool.acquire()
                lastv = None
                for gg in range(2):
                    gi, slot, dv = wload(owT, (m * MDI + gg * 8) * 2)
                    first = [True]
                    for xi, wj in ((0, 0), (1, 0), (0, 1)):
                        for k8 in range(8):
                            k = gg * 8 + k8
                            w0 = []
                            if first[0]:
                                w0 = [dv]
                                if gg == 0:
                                    w0.append(ptok)
                                    w0.append((sV, ysp))
                                first[0] = False
                            xs = yh(k) if xi == 0 else yl(k)
                            st = (gg == 0 and xi == 0 and wj == 0 and k8 == 0)
                            sp = (gg == 1 and xi == 0 and wj == 1 and k8 == 7)
                            lastv = op("tensor",
                                       lambda e, k8=k8, wj=wj, xs=xs,
                                       slot=slot, pslot=pslot, st=st, sp=sp:
                                       e.matmul(
                                           bank(PPB[pslot], 0, Tc),
                                           wtile(slot, k8 * 2 + wj), xs,
                                           start=st, stop=sp),
                                       waits=w0,
                                       inc=(sT if (xi == 0 and wj == 1
                                                   and k8 == 7) else None))
                    ring_last[gi] = lastv
                out_mm_last[c] = lastv
                ew = [(sT, lastv)]
                if m == 0:
                    ew.append((sV, ym_ev[c]))
                    if (c + 1) in tr_last:
                        ew.append((sT, tr_last[c + 1]))
                outev[m] = op("scalar", lambda e, m=m, pslot=pslot:
                              e.activation(
                                  out=a(A_AT + m * Tc, Tc),
                                  in_=bank(PPB[pslot], 0, Tc),
                                  func=Act.Copy), waits=ew, inc=sS)
                pp_pool.release(pslot, sS, outev[m])
            at_tap = (tap("at", A_AT, MDM * Tc, (sS, outev[MDM - 1]))
                      if c == TAPC else None)

            relu_all = None
            for m in range(MFF):
                if m % 2 == 0:
                    gi, slot, dv = wload(f1T, m * KDM)
                pslot, ptok = pp_pool.acquire()
                lastv = None
                first = [True]
                for xi in range(2):
                    for k in range(KDM):
                        w0 = []
                        if first[0]:
                            w0 = [dv, ptok]
                            first[0] = False
                        xs = xnfh(c, k) if xi == 0 else xnfl(c, k)
                        st = (xi == 0 and k == 0)
                        sp = (xi == 1 and k == KDM - 1)
                        lastv = op("tensor",
                                   lambda e, m=m, k=k, xs=xs, slot=slot,
                                   pslot=pslot, st=st, sp=sp: e.matmul(
                                       bank(PPB[pslot], 0, Tc),
                                       wtile(slot, (m % 2) * KDM + k), xs,
                                       start=st, stop=sp),
                                   waits=w0, inc=(sT if sp else None))
                if m % 2 == 1:
                    ring_last[gi] = lastv
                ffn1_last[c] = lastv
                hslot, htok = hf_pool.acquire()
                ew = [(sT, lastv), htok]
                if m == 0:
                    ew.append((sV, ym_ev[c]))
                    if debug and c == TAPC and dttap[0] is not None:
                        ew.append((sTAP, dttap[0]))
                hf = op("scalar", lambda e, m=m, pslot=pslot, hslot=hslot:
                        e.activation(
                            out=a(HFC[hslot], Tc),
                            in_=bank(PPB[pslot], 0, Tc),
                            func=Act.Relu, bias=a(A_B1 + m, 1)),
                        waits=ew, inc=sS)
                pp_pool.release(pslot, sS, hf)
                hcv = op("scalar", lambda e, m=m, hslot=hslot: e.activation(
                    out=hh(m), in_=a(HFC[hslot], Tc), func=Act.Copy),
                    waits=[(sS, hf)], inc=sS)
                relu_all = op("vector", lambda e, m=m, hslot=hslot:
                              e.tensor_sub(
                                  out=hl(m), in0=a(HFC[hslot], Tc),
                                  in1=hh(m)), waits=[(sS, hcv)], inc=sV)
                hf_pool.release(hslot, sV, relu_all)
            relu_ev[c] = relu_all

            fe = {}
            for m in range(MDM):
                pslot, ptok = pp_pool.acquire()
                lastv = None
                for gg in range(2):
                    gi, slot, dv = wload(f2T, m * MFF + gg * G)
                    first = [True]
                    for xi in range(2):
                        for kk in range(G):
                            k = gg * G + kk
                            w0 = []
                            if first[0]:
                                w0 = [dv]
                                if gg == 0:
                                    w0.append(ptok)
                                    if m == 0:
                                        w0.append((sV, relu_all))
                                first[0] = False
                            xs = hh(k) if xi == 0 else hl(k)
                            st = (gg == 0 and xi == 0 and kk == 0)
                            sp = (gg == 1 and xi == 1 and kk == G - 1)
                            lastv = op("tensor",
                                       lambda e, kk=kk, xs=xs, slot=slot,
                                       pslot=pslot, st=st, sp=sp: e.matmul(
                                           bank(PPB[pslot], 0, Tc),
                                           wtile(slot, kk), xs,
                                           start=st, stop=sp),
                                       waits=w0,
                                       inc=(sT if (xi == 1 and kk == G - 1)
                                            else None))
                    ring_last[gi] = lastv
                ffn2_last[c] = lastv
                vw = [(sT, lastv), (sS, outev[m])]
                if c == TAPC and m == 0 and at_tap is not None:
                    vw.append((sTAP, at_tap))
                fe[m] = op("vector",
                           lambda e, m=m, pslot=pslot: e.scalar_tensor_tensor(
                               out=a(A_AT + m * Tc, Tc),
                               in0=bank(PPB[pslot], 0, Tc),
                               scalar=a(A_B2 + m, 1), in1=a(A_AT + m * Tc, Tc),
                               op0=Alu.add, op1=Alu.add), waits=vw, inc=sV)
                pp_pool.release(pslot, sV, fe[m])
            if c == TAPC:
                tap("sf", A_AT, MDM * Tc, (sV, fe[MDM - 1]))

            sx = A_SX + (c % 2) * 2 * DM
            for bt in range(4):
                tt, dq = bt // 2, bt % 2
                slot, tok = tp_pool.acquire()
                last = None
                for j in range(4):
                    dc = dq * 4 + j
                    w0 = []
                    if j == 0:
                        w0 = [tok, (sV, fe[dq * 4 + 3])]
                    last = op("tensor",
                              lambda e, tt=tt, dc=dc, j=j, slot=slot: e.matmul(
                                  bank(TPB[slot], j * 128, 128),
                                  a(A_AT + dc * Tc + tt * 128, 128),
                                  a(A_ID, 128),
                                  start=(j == 0), stop=(j == 3),
                                  is_transpose=True),
                              waits=w0, inc=(sT if j == 3 else None))
                ft_last[c] = last
                vw = [(sT, last)]
                if bt == 0 and c > 0:
                    vw.append(out_store_tok[c - 1])
                fev_last[c] = op("vector",
                                 lambda e, tt=tt, dq=dq, slot=slot,
                                 sx=sx: e.tensor_add(
                                     out=a(A_OUT + tt * DM + dq * 512, 512),
                                     in0=bank(TPB[slot], 0, 512),
                                     in1=a(sx + tt * DM + dq * 512, 512)),
                                 waits=vw, inc=sV)
                tp_pool.release(slot, sV, fev_last[c])

            out_store_tok[c] = dstore(lambda e, c=c: e.dma_start(
                out=out_d[c * Tc:(c + 1) * Tc, :].rearrange(
                    "(t p) d -> p t d", p=128),
                in_=a(A_OUT, 2 * DM).rearrange("p (t d) -> p t d", d=DM)),
                waits=[(sV, fev_last[c])])

        def emit_front_b(c, prolog=False):
            zev = None
            for m in range(MDI, MIP):
                gi, slot, dv = wload(ipT, m * 2 * KDM)
                pslot, ptok = pp_pool.acquire()
                lastv = None
                first = [True]
                for xi, wj in ((0, 0), (1, 0), (0, 1)):
                    for k in range(KDM):
                        w0 = []
                        if first[0]:
                            w0 = [dv, ptok]
                            if m == MDI and prolog:
                                w0.append((sS, xnf_ready[c]))
                            first[0] = False
                        xs = xnfh(c, k) if xi == 0 else xnfl(c, k)
                        st = (xi == 0 and wj == 0 and k == 0)
                        sp = (xi == 0 and wj == 1 and k == KDM - 1)
                        lastv = op("tensor",
                                   lambda e, k=k, wj=wj, xs=xs, slot=slot,
                                   pslot=pslot, st=st, sp=sp: e.matmul(
                                       bank(PPB[pslot], 0, Tc),
                                       wtile(slot, k * 2 + wj), xs,
                                       start=st, stop=sp),
                                   waits=w0, inc=(sT if sp else None))
                ring_last[gi] = lastv
                ew = [(sT, lastv)]
                if m == MDI and not prolog:
                    ew.append((sT, ffn2_last[c - 1]))
                zev = op("scalar", lambda e, m=m, pslot=pslot: e.activation(
                    out=a(A_Z + (m - MDI) * Tc, Tc),
                    in_=bank(PPB[pslot], 0, Tc), func=Act.Silu),
                    waits=ew, inc=sS)
                pp_pool.release(pslot, sS, zev)
            z_ev[c] = zev
            if c == TAPC:
                tap("z", A_Z, MDI * Tc, (sS, zev))

            dt_ev = None
            for m in range(MDI):
                pslot, ptok = pp_pool.acquire()
                lastv = None
                for gg in range(2):
                    gi, slot, dv = wload(dtT, (m * MDI + gg * 8) * 2)
                    first = [True]
                    for xi, wj in ((0, 0), (1, 0), (0, 1)):
                        for k8 in range(8):
                            k = gg * 8 + k8
                            w0 = []
                            if first[0]:
                                w0 = [dv]
                                if gg == 0:
                                    w0.append(ptok)
                                    if m == 0:
                                        w0.append((sV, ucv_all[c]))
                                first[0] = False
                            xs = uh(k) if xi == 0 else ul(k)
                            st = (gg == 0 and xi == 0 and wj == 0
                                  and k8 == 0)
                            sp = (gg == 1 and xi == 0 and wj == 1
                                  and k8 == 7)
                            lastv = op("tensor",
                                       lambda e, k8=k8, wj=wj, xs=xs,
                                       slot=slot, pslot=pslot, st=st, sp=sp:
                                       e.matmul(
                                           bank(PPB[pslot], 0, Tc),
                                           wtile(slot, k8 * 2 + wj), xs,
                                           start=st, stop=sp),
                                       waits=w0,
                                       inc=(sT if (xi == 0 and wj == 1
                                                   and k8 == 7) else None))
                    ring_last[gi] = lastv
                ew = [(sT, lastv)]
                if m == 0 and not prolog:
                    ew.append((sT, ffn2_last[c - 1]))
                dt_ev = op("scalar", lambda e, m=m, pslot=pslot: e.activation(
                    out=a(A_DT + m * Tc, Tc), in_=bank(PPB[pslot], 0, Tc),
                    func=Act.Sigmoid, bias=a(A_DB + m, 1)),
                    waits=ew, inc=sS)
                pp_pool.release(pslot, sS, dt_ev)
            dt_all[c] = dt_ev

            pslot, ptok = pp_pool.acquire()
            lastv = None
            first = [True]
            for half in range(2):
                for xi, wj in ((0, 0), (1, 0), (0, 1)):
                    for k in range(MDI):
                        w0 = []
                        if first[0]:
                            w0 = [ptok]
                            first[0] = False
                        wt = ab(A_BCW + ((half * 2 + wj) * MDI + k) * 8, N)
                        xs = uh(k) if xi == 0 else ul(k)
                        st = (half == 0 and xi == 0 and wj == 0 and k == 0)
                        sp = (half == 1 and xi == 0 and wj == 1
                              and k == MDI - 1)
                        lastv = op("tensor",
                                   lambda e, wt=wt, xs=xs, half=half,
                                   pslot=pslot, st=st, sp=sp: e.matmul(
                                       bank(PPB[pslot], half * Tc, Tc, pn=N),
                                       wt, xs, start=st, stop=sp),
                                   waits=w0, inc=(sT if sp else None))
            ew = [(sT, lastv)]
            if not prolog:
                ew.append((sT, rep_last[c - 1]))
            bc_ev[c] = op("scalar", lambda e, pslot=pslot: e.activation(
                out=ab(A_SBC, 2 * Tc, pn=N),
                in_=bank(PPB[pslot], 0, 2 * Tc, pn=N),
                func=Act.Copy), waits=ew, inc=sS)
            bc_ev_l[c] = op("vector", lambda e, pslot=pslot: e.tensor_sub(
                out=ab(A_SBC + Tc, 2 * Tc, pn=N),
                in0=bank(PPB[pslot], 0, 2 * Tc, pn=N),
                in1=ab(A_SBC, 2 * Tc, pn=N)),
                waits=[(sS, bc_ev[c])], inc=sV)
            pp_pool.release(pslot, sV, bc_ev_l[c])
            if c == TAPC:
                tap("sbch", A_SBC, Tc, (sV, bc_ev_l[c]), pn=N, bf=True)
                tap("sbcl", A_SBC + Tc, Tc, (sV, bc_ev_l[c]), pn=N, bf=True)

            clw = [(sS, dt_all[c])]
            if not prolog:
                clw.append((sT, out_mm_last[c - 1]))
            cl = op("vector", lambda e: e.tensor_scalar(
                out=a(A_DT, MDI * Tc), in0=a(A_DT, MDI * Tc), scalar1=DT_MIN,
                scalar2=1.0, op0=Alu.max, op1=Alu.min), waits=clw, inc=sV)
            dtp = None
            for cc in range(MDI):
                op("vector", lambda e, cc=cc: e.tensor_tensor(
                    out=a(A_B, Tc), in0=a(A_DT + cc * Tc, Tc), in1=uh(cc),
                    op=Alu.mult), waits=([(sV, cl)] if cc == 0 else []))
                duv = op("vector", lambda e, cc=cc: e.tensor_tensor(
                    out=a(A_SS, Tc), in0=a(A_DT + cc * Tc, Tc), in1=ul(cc),
                    op=Alu.mult), inc=sV)
                op("vector", lambda e, cc=cc: e.tensor_add(
                    out=a(A_DTU + cc * Tc, Tc), in0=a(A_B, Tc),
                    in1=a(A_SS, Tc)), waits=[(sV, duv)])
                yiv = op("vector", lambda e, cc=cc: e.tensor_scalar(
                    out=a(A_Y + cc * Tc, Tc), in0=uh(cc),
                    scalar1=a(A_DV + cc, 1), scalar2=None, op0=Alu.mult),
                    inc=sV)
                dtp = op("vector", lambda e, cc=cc: e.scalar_tensor_tensor(
                    out=a(A_Y + cc * Tc, Tc), in0=ul(cc),
                    scalar=a(A_DV + cc, 1), in1=a(A_Y + cc * Tc, Tc),
                    op0=Alu.mult, op1=Alu.add), waits=[(sV, yiv)],
                    inc=(sV if cc == MDI - 1 else None))
            dtp_ev[c] = dtp
            if c == TAPC:
                dttap[0] = tap("dt", A_DT, MDI * Tc, (sV, dtp_ev[c]))
                tap("dtu", A_DTU, MDI * Tc, (sV, dtp_ev[c]))

        emit_xload(0, extra=prep_store_toks)
        if NCH > 1:
            emit_xload(1)
        emit_ln(0)
        for iu, u in enumerate(build_units(0)):
            if iu == 20:
                flush_pend(99)
            u(0)
        flush_pend(99)
        if debug and TAPC == 0:
            utap[0] = tap("u", A_UH, MDI * 128, (sV, ucv_all[0]), bf=True)
            tap("ulo", A_UL, MDI * 128, (sV, ucv_all[0]), bf=True)
            tap("xch", A_XC, MDI * XCW, (sV, xc_all[0]), bf=True)
            tap("xcl", A_XC + MDI * XCW, MDI * XCW, (sV, xc_all[0]), bf=True)
        emit_front_b(0, prolog=True)

        for c in range(NCH):
            if c + 1 < NCH:
                emit_ln(c + 1)
                units = build_units(c + 1)
            else:
                units = []
            sl = emit_scan(c, units)
            if c == TAPC:
                tap("y", A_Y, MDI * Tc, (sG, sl))
            emit_tail(c)
            if c + 1 < NCH:
                emit_front_b(c + 1)
                if debug and c + 1 == TAPC:
                    utap[0] = tap("u", A_UH, MDI * 128, (sV, ucv_all[c + 1]),
                                  bf=True)
                    tap("ulo", A_UL, MDI * 128, (sV, ucv_all[c + 1]), bf=True)
            if c + 2 < NCH:
                emit_xload(c + 2, extra=[(sV, fev_last[c])])

        op("gpsimd", None, waits=dall())

        used = set()
        for _eng in sch.ops:
            for _fn, _w, _i, _ib in sch.ops[_eng]:
                for _s, _v in _w:
                    used.add((id(_s), _v))
        oldc, newc, vmap = {}, {}, {}
        for _eng in ("tensor", "vector", "scalar", "sync", "gpsimd"):
            for oi, (_fn, _w, _i, _ib) in enumerate(sch.ops[_eng]):
                if _i is None:
                    continue
                k = id(_i)
                ov = oldc.get(k, 0) + _ib
                oldc[k] = ov
                if (k, ov) in used:
                    nv = newc.get(k, 0) + _ib
                    newc[k] = nv
                    vmap[(k, ov)] = nv
                else:
                    sch.ops[_eng][oi] = (_fn, _w, None, _ib)

        def replay(eng):
            items = sch.ops[eng]

            def body(e, items=items):
                seen = {}
                for fn, waits, inc, inc_by in items:
                    for s, v in waits:
                        v = vmap[(id(s), v)]
                        if seen.get(id(s), 0) < v:
                            e.wait_ge(s, v)
                            seen[id(s)] = v
                    if fn is None:
                        continue
                    ins = fn(e)
                    if inc is not None:
                        ins.then_inc(inc, inc_by)
            return body

        block.sync(replay("sync"))
        block.tensor(replay("tensor"))
        block.vector(replay("vector"))
        block.scalar(replay("scalar"))
        block.gpsimd(replay("gpsimd"))

    return nc


_NC_CACHE = {}


def kernel(L0_attn_x, L0_attn_in_proj_w, L0_attn_conv_w, L0_attn_conv_b,
           L0_attn_dt_w, L0_attn_dt_b, L0_attn_A_log, L0_attn_D, L0_attn_B_w,
           L0_attn_C_w, L0_attn_out_w, L0_ffn_W1, L0_ffn_b1, L0_ffn_W2,
           L0_ffn_b2):
    S = L0_attn_x.shape[1]
    if S not in _NC_CACHE:
        _NC_CACHE[S] = build_program(S)
    nc = _NC_CACHE[S]
    w = {
        "in_proj_w": np.ascontiguousarray(L0_attn_in_proj_w, np.float32),
        "conv_w": np.ascontiguousarray(L0_attn_conv_w, np.float32),
        "conv_b": np.ascontiguousarray(L0_attn_conv_b, np.float32),
        "dt_w": np.ascontiguousarray(L0_attn_dt_w, np.float32),
        "dt_b": np.ascontiguousarray(L0_attn_dt_b, np.float32),
        "A_log": np.ascontiguousarray(L0_attn_A_log, np.float32),
        "Dv": np.ascontiguousarray(L0_attn_D, np.float32),
        "B_w": np.ascontiguousarray(L0_attn_B_w, np.float32),
        "C_w": np.ascontiguousarray(L0_attn_C_w, np.float32),
        "out_w": np.ascontiguousarray(L0_attn_out_w, np.float32),
        "W1": np.ascontiguousarray(L0_ffn_W1, np.float32),
        "b1": np.ascontiguousarray(L0_ffn_b1, np.float32),
        "W2": np.ascontiguousarray(L0_ffn_W2, np.float32),
        "b2": np.ascontiguousarray(L0_ffn_b2, np.float32),
    }
    nb = L0_attn_x.shape[0]
    in_maps = [dict(w, x=np.ascontiguousarray(L0_attn_x[b], np.float32))
               for b in range(nb)]
    res = run_bass_kernel_spmd(nc, in_maps, core_ids=list(range(nb)))
    return np.stack([r["out"] for r in res.results], axis=0)


# version 10
# speedup vs baseline: 1.2511x; 1.2343x over previous
import numpy as np
from contextlib import ExitStack

import concourse.bass as bass
import concourse.mybir as mybir
from concourse.bass_utils import run_bass_kernel_spmd

F32 = mybir.dt.float32
BF16 = mybir.dt.bfloat16
Alu = mybir.AluOpType
Act = mybir.ActivationFunctionType
AxX = mybir.AxisListType.X

DM, DI, N, FF, TAPS = 1024, 2048, 16, 4096, 4
Tc = 256
KDM = DM // 128
MDI = DI // 128
MIP = 2 * DI // 128
MFF = FF // 128
MDM = DM // 128
G = 16

TAPC = 0
LN_EPS = 1e-5
FFN_EPS = 1e-5
SCAN_EPS = 1e-6
DT_MIN = 1e-4


class Pool:

    def __init__(self, n):
        self.tok = [None] * n
        self.i = 0
        self.n = n

    def acquire(self):
        s = self.i % self.n
        self.i += 1
        t = self.tok[s]
        self.tok[s] = None
        return s, t

    def release(self, s, sem, val):
        self.tok[s] = (sem, val)


class Sched:
    def __init__(self):
        self.ops = {e: [] for e in ("tensor", "vector", "scalar", "sync", "gpsimd")}
        self.cnt = {}

    def mark(self, eng, name):
        self.ops[eng].append(name)

    def op(self, eng, fn, waits=(), inc=None, inc_by=1):
        w = []
        for t in waits:
            if t is None:
                continue
            s, v = t
            assert v <= self.cnt.get(id(s), 0), f"wait on future event ({eng})"
            if v > 0:
                w.append((s, v))
        val = None
        if inc is not None:
            self.cnt[id(inc)] = self.cnt.get(id(inc), 0) + inc_by
            val = self.cnt[id(inc)]
        self.ops[eng].append((fn, w, inc, inc_by))
        return val


def build_program(S=4096, debug=False):
    NCH = S // Tc
    nc = bass.Bass("TRN2", target_bir_lowering=False, debug=False)
    dbg = {}

    x_d = nc.dram_tensor("x", [S, DM], F32, kind="ExternalInput").ap()
    w_ip = nc.dram_tensor("in_proj_w", [2 * DI, DM], F32, kind="ExternalInput").ap()
    w_cv = nc.dram_tensor("conv_w", [DI, DI, TAPS], F32, kind="ExternalInput").ap()
    b_cv = nc.dram_tensor("conv_b", [DI], F32, kind="ExternalInput").ap()
    w_dt = nc.dram_tensor("dt_w", [DI, DI], F32, kind="ExternalInput").ap()
    b_dt = nc.dram_tensor("dt_b", [DI], F32, kind="ExternalInput").ap()
    a_log = nc.dram_tensor("A_log", [DI, N], F32, kind="ExternalInput").ap()
    d_v = nc.dram_tensor("Dv", [DI], F32, kind="ExternalInput").ap()
    w_b = nc.dram_tensor("B_w", [N, DI], F32, kind="ExternalInput").ap()
    w_c = nc.dram_tensor("C_w", [N, DI], F32, kind="ExternalInput").ap()
    w_ow = nc.dram_tensor("out_w", [DM, DI], F32, kind="ExternalInput").ap()
    w_f1 = nc.dram_tensor("W1", [FF, DM], F32, kind="ExternalInput").ap()
    b_f1 = nc.dram_tensor("b1", [FF], F32, kind="ExternalInput").ap()
    w_f2 = nc.dram_tensor("W2", [DM, FF], F32, kind="ExternalInput").ap()
    b_f2 = nc.dram_tensor("b2", [DM], F32, kind="ExternalInput").ap()
    out_d = nc.dram_tensor("out", [S, DM], F32, kind="ExternalOutput").ap()

    ipT = nc.dram_tensor("ipT", [128, 2 * MIP * KDM * 128], BF16).ap()
    cvT = nc.dram_tensor("cvT", [128, 2 * MDI * TAPS * MDI * 128], BF16).ap()
    dtT = nc.dram_tensor("dtT", [128, 2 * MDI * MDI * 128], BF16).ap()
    owT = nc.dram_tensor("owT", [128, 2 * MDM * MDI * 128], BF16).ap()
    f1T = nc.dram_tensor("f1T", [128, MFF * KDM * 128], BF16).ap()
    f2T = nc.dram_tensor("f2T", [128, MDM * MFF * 128], BF16).ap()

    pos = [0]

    def seg(w):
        c = pos[0]
        pos[0] += w
        return c

    A_SX = seg(2 * 2 * DM)
    A_XNT = seg(2 * DM)
    A_OUT = seg(2 * DM)
    A_AT = seg(2 * DM)
    A_XNF = seg(2 * 2 * KDM * 128)
    XCW = 130
    A_XC = seg(2 * MDI * XCW)
    A_Z = seg(MDI * Tc)
    A_UH = seg(MDI * 128)
    A_UL = seg(MDI * 128)
    A_DT = seg(MDI * Tc)
    A_DTU = seg(MDI * Tc)
    A_Y = seg(MDI * Tc)
    A_WR = seg(4 * G * 64)
    A_DA = seg(2 * Tc)
    A_B = seg(2 * Tc)
    A_SS = seg(2 * Tc)
    A_P = seg(2 * Tc)
    A_ST = seg(Tc)
    A_EPS = seg(1)
    A_ASB = seg(Tc)
    A_BCW = seg(2 * 2 * MDI * 8)
    A_SBC = seg(2 * Tc)
    A_SEL = seg(N * 64)
    A_CB = seg(MDI)
    A_DB = seg(MDI)
    A_DV = seg(MDI)
    A_B1 = seg(MFF)
    A_B2 = seg(MDM)
    A_ID = seg(128)
    A_IDB = seg(64)
    A_ONE = seg(1)
    A_ONR = seg(128)
    A_STAT = seg(16)
    A_GAM = seg(8)
    ACOLS = ((pos[0] + 127) // 128) * 128
    assert ACOLS <= 49152, f"arena too big: {ACOLS}"

    P_NAT = A_Z
    assert A_DTU - A_Z >= 8192 - 4096
    assert A_Y - A_Z >= 8192, "P_NAT needs 8192 contiguous cols"
    P_NH = A_DTU
    sch = Sched()
    P_NHB = A_DTU
    P_NLB = A_Y

    with ExitStack() as est:
        en = est.enter_context
        ar_h = en(nc.sbuf_tensor("arena", [128, ACOLS], F32))
        ps_h = en(nc.psum_tensor("ps", [128, 4096], F32))
        sD = [en(nc.semaphore(f"sD{i}")) for i in range(6)]
        sW = [en(nc.semaphore(f"sW{i}")) for i in range(2)]
        sT = en(nc.semaphore("sT"))
        sS = en(nc.semaphore("sS"))
        sV = en(nc.semaphore("sV"))
        sG = en(nc.semaphore("sG"))
        sTAP = en(nc.semaphore("sTAP"))
        block = en(nc.Block())

        def a(c0, w, p0=0, pn=128):
            return ar_h[p0:p0 + pn, c0:c0 + w]

        def ab(c0, wbf, p0=0, pn=128):
            return ar_h[p0:p0 + pn, c0:c0 + wbf // 2].bitcast(BF16)

        def bank(k, c0=0, w=512, p0=0, pn=128):
            return ps_h[p0:p0 + pn, k * 512 + c0:k * 512 + c0 + w]

        def bankb(k, c0bf=0, wbf=1024, p0=0, pn=128):
            return ps_h[p0:p0 + pn,
                        k * 512 + c0bf // 2:
                        k * 512 + (c0bf + wbf) // 2].bitcast(BF16)

        TPB, PPB, REPB = (0, 1), (4, 5, 6, 7), (2, 3)
        tp_pool, pp_pool, rep_pool = Pool(2), Pool(4), Pool(2)
        da_pool, hf_pool = Pool(2), Pool(2)
        HFC = (A_B, A_SS)

        op = sch.op

        dseq, dcnt = [0], {}
        wseq, wcnt = [0], {}

        def dload(fn, waits=()):
            i = dseq[0] % len(sD)
            dseq[0] += 1
            w = list(waits)
            prev = dcnt.get(i, 0)
            if prev:
                w.append((sD[i], prev))
            v = op("sync", fn, waits=w, inc=sD[i], inc_by=16)
            dcnt[i] = v
            return (sD[i], v)

        def dstore(fn, waits=()):
            i = wseq[0] % len(sW)
            wseq[0] += 1
            w = list(waits)
            prev = wcnt.get(i, 0)
            if prev:
                w.append((sW[i], prev))
            v = op("gpsimd", fn, waits=w, inc=sW[i], inc_by=16)
            wcnt[i] = v
            return (sW[i], v)

        def dall():
            return [(sW[i], wcnt[i]) for i in range(len(sW)) if wcnt.get(i)]

        def tap(name, c0, w, after, pn=128, bf=False):
            if not debug:
                return None
            if bf:
                t = nc.dram_tensor("dbg_" + name, [pn, 2 * w], BF16,
                                   kind="ExternalOutput").ap()
                src = ab(c0, 2 * w, pn=pn)
            else:
                t = nc.dram_tensor("dbg_" + name, [pn, w], F32,
                                   kind="ExternalOutput").ap()
                src = a(c0, w, pn=pn)
            dbg[name] = t
            return op("gpsimd", lambda e, t=t, src=src: e.dma_start(
                out=t, in_=src), waits=[after], inc=sTAP, inc_by=16)

        g_iota = op("gpsimd", lambda e: e.iota(
            out=a(A_ID, 128), pattern=[[-1, 128]], base=0, channel_multiplier=1,
            allow_small_or_imprecise_dtypes=True), inc=sG)
        v_id = op("vector", lambda e: e.tensor_scalar(
            out=a(A_ID, 128), in0=a(A_ID, 128), scalar1=0.0, scalar2=None,
            op0=Alu.is_equal), waits=[(sG, g_iota)], inc=sV)
        v_idb = op("vector", lambda e: e.tensor_copy(
            out=ab(A_IDB, 128), in_=a(A_ID, 128)), waits=[(sV, v_id)], inc=sV)
        g_sel = op("gpsimd", lambda e: e.iota(
            out=a(A_AT, N * 128).rearrange("p (n j) -> p n j", j=128),
            pattern=[[-1, N], [0, 128]], base=0, channel_multiplier=1,
            allow_small_or_imprecise_dtypes=True), inc=sG)
        v_self = op("vector", lambda e: e.tensor_scalar(
            out=ab(A_SEL, N * 128), in0=a(A_AT, N * 128),
            scalar1=0.0, scalar2=None, op0=Alu.is_equal),
            waits=[(sG, g_sel)], inc=sV)
        op("vector", lambda e: e.memset(a(A_SBC, 2 * Tc), 0.0), inc=sV)
        op("vector", lambda e: e.memset(a(A_ONE, 1), 1.0), inc=sV)
        v_onr = op("vector", lambda e: e.memset(a(A_ONR, 128, pn=1), 1.0), inc=sV)
        op("vector", lambda e: e.memset(a(A_XC, 2 * MDI * XCW), 0.0), inc=sV)
        op("vector", lambda e: e.memset(a(A_ST, Tc), 0.0), inc=sV)
        op("vector", lambda e: e.memset(a(A_EPS, 1), SCAN_EPS), inc=sV)

        btoks = []
        for src, dst, wd in ((b_cv, A_CB, MDI), (b_dt, A_DB, MDI),
                             (d_v, A_DV, MDI), (b_f1, A_B1, MFF),
                             (b_f2, A_B2, MDM)):
            btoks.append(dload(lambda e, src=src, dst=dst, wd=wd: e.dma_start(
                out=a(dst, wd), in_=src.rearrange("(c p) -> p c", p=128),
                allow_slow_non_contiguous=True)))
        d_alog = dload(lambda e: e.dma_start(
            out=a(A_ASB, Tc).rearrange("p (c n) -> p c n", n=N),
            in_=a_log.rearrange("(c p) n -> p c n", p=128)))
        s_exp = op("scalar", lambda e: e.activation(
            out=a(A_ASB, Tc), in_=a(A_ASB, Tc), func=Act.Exp),
            waits=[d_alog] + btoks, inc=sS)
        v_asb = op("vector", lambda e: e.tensor_scalar(
            out=a(A_ASB, Tc), in0=a(A_ASB, Tc), scalar1=-1.0, scalar2=None,
            op0=Alu.mult), waits=[(sS, s_exp)] + btoks, inc=sV)
        tap("asb", A_ASB, Tc, (sV, v_asb))

        nat_tok = [None]
        str_tok = [None]
        tr_tok = [None]
        n_prep_stores = [0]

        last = None
        for wi, wsrc in enumerate((w_b, w_c)):
            d0 = dload(lambda e, w=wsrc: e.dma_start(
                out=a(P_NAT, DI, pn=N), in_=w), waits=[nat_tok[0]])
            sh = op("scalar", lambda e: e.activation(
                out=ab(P_NHB, DI, pn=N), in_=a(P_NAT, DI, pn=N),
                func=Act.Copy), waits=[d0], inc=sS)
            sl = op("vector", lambda e: e.tensor_sub(
                out=ab(P_NLB, DI, pn=N), in0=a(P_NAT, DI, pn=N),
                in1=ab(P_NHB, DI, pn=N)), waits=[(sS, sh)], inc=sV)
            for hl, srcb in ((0, P_NHB), (1, P_NLB)):
                for half in range(2):
                    slot, tok = tp_pool.acquire()
                    for j in range(8):
                        k = half * 8 + j
                        last = op("tensor",
                                  lambda e, k=k, j=j, slot=slot, srcb=srcb:
                                  e.matmul(
                                      bankb(TPB[slot], j * N, N),
                                      ab(srcb + k * 64, 128, pn=N),
                                      ab(A_IDB, N, pn=N),
                                      start=(j == 0), stop=(j == 7),
                                      is_transpose=True),
                                  waits=([(sV, sl), (sV, v_idb), tok]
                                         if j == 0 else []),
                                  inc=(sT if j == 7 else None))
                    ev = op("scalar",
                            lambda e, slot=slot, wi=wi, hl=hl, half=half:
                            e.activation(
                                out=ab(A_BCW +
                                       ((wi * 2 + hl) * MDI + half * 8) * 8,
                                       8 * N),
                                in_=bankb(TPB[slot], 0, 8 * N),
                                func=Act.Copy),
                            waits=[(sT, last)], inc=sS)
                    tp_pool.release(slot, sS, ev)
            nat_tok[0] = (sT, last)
        tr_tok[0] = (sT, last)

        def prep_weight(nat_src_fn, nblocks, ktiles, scratch, width,
                        quant=None, strided_taps=False):
            if quant is not None and qtm2_tok[0] is None:
                qtm2_tok[0] = tr_tok[0]
            for m in range(nblocks):
                d0 = dload(lambda e, m=m: e.dma_start(
                    out=a(P_NAT, width), in_=nat_src_fn(m)),
                    waits=[nat_tok[0]])
                if quant is None:
                    sh = op("scalar", lambda e, width=width: e.activation(
                        out=ab(P_NHB, width), in_=a(P_NAT, width),
                        func=Act.Copy), waits=[d0, tr_tok[0]], inc=sS)
                    sl = op("vector", lambda e, width=width: e.tensor_sub(
                        out=ab(P_NLB, width), in0=a(P_NAT, width),
                        in1=ab(P_NHB, width)), waits=[(sS, sh)], inc=sV)
                    nat_tok[0] = (sV, sl)
                    gate = (sV, sl)
                    ntile2 = 2 * ktiles
                else:
                    vq = quant(m, d0)
                    nat_tok[0] = (sV, vq)
                    gate = (sV, vq)
                    ntile2 = ktiles
                last_ev = None
                last = None
                nb = (ntile2 + 7) // 8
                for bt in range(nb):
                    slot, tok = tp_pool.acquire()
                    nin = min(8, ntile2 - bt * 8)
                    for j in range(nin):
                        t2 = bt * 8 + j
                        if quant is None:
                            k, hl = t2 // 2, t2 % 2
                            srcb = P_NHB if hl == 0 else P_NLB
                        else:
                            k, srcb = t2, P_NLB
                        if strided_taps:
                            tap_, i = k // MDI, k % MDI
                            src = ab(srcb, width).rearrange(
                                "p (i t) -> p i t", t=TAPS)[
                                :, i * 128:(i + 1) * 128, tap_]
                        else:
                            src = ab(srcb + k * 64, 128)
                        w0 = [tok] + ([gate, (sV, v_idb)]
                                      if (bt == 0 and j == 0) else [])
                        last = op("tensor",
                                  lambda e, src=src, slot=slot, j=j, nin=nin:
                                  e.matmul(
                                      bankb(TPB[slot], j * 128, 128), src,
                                      ab(A_IDB, 128), start=(j == 0),
                                      stop=(j == nin - 1), is_transpose=True),
                                  waits=(w0 if j == 0 else []),
                                  inc=(sT if j == nin - 1 else None))
                    ev = op("scalar",
                            lambda e, slot=slot, bt=bt, nin=nin: e.activation(
                                out=ab(P_STR + bt * 512, nin * 128),
                                in_=bankb(TPB[slot], 0, nin * 128),
                                func=Act.Copy),
                            waits=[(sT, last)] + ([str_tok[0]]
                                                  if bt == 0 else []),
                            inc=sS)
                    tp_pool.release(slot, sS, ev)
                    last_ev = ev
                if quant is None:
                    tr_tok[0] = (sT, last)
                else:
                    qtm2_tok[0] = (sT, last)
                str_tok[0] = dstore(
                    lambda e, m=m, nt=ntile2, scr=scratch: e.dma_start(
                        out=scr[:, m * nt * 128:(m + 1) * nt * 128],
                        in_=ab(P_STR, nt * 128)),
                    waits=[(sS, last_ev)])
                n_prep_stores[0] += 1

        P_STR = A_SX

        prep_weight(lambda m: w_ip[m * 128:(m + 1) * 128, :], MIP, KDM, ipT, DM)
        prep_weight(lambda m: w_cv[m * 128:(m + 1) * 128, :, :].rearrange(
            "p i t -> p (i t)"), MDI, TAPS * MDI, cvT, DI * TAPS,
            strided_taps=True)
        prep_weight(lambda m: w_dt[m * 128:(m + 1) * 128, :], MDI, MDI, dtT, DI)
        prep_weight(lambda m: w_ow[m * 128:(m + 1) * 128, :], MDM, MDI, owT, DI)

        def gamma_pass(wsrc, nblocks, width, gcol):
            op("vector", lambda e: e.memset(a(gcol, 1), 0.0), inc=sV)
            rv = None
            for m in range(nblocks):
                d0 = dload(lambda e, m=m, w=wsrc: e.dma_start(
                    out=a(P_NAT, width), in_=w[m * 128:(m + 1) * 128, :]),
                    waits=[nat_tok[0]])
                gv = op("vector", lambda e, width=width: e.reduce_sum(
                    out=a(gcol + 1, 1), in_=a(P_NAT, width), axis=AxX,
                    apply_absolute_value=True), waits=[d0], inc=sV)
                rv = op("vector", lambda e: e.tensor_add(
                    out=a(gcol, 1), in0=a(gcol, 1), in1=a(gcol + 1, 1)),
                    waits=[(sV, gv)], inc=sV)
                nat_tok[0] = (sV, rv)
            mm = op("tensor", lambda e: e.matmul(
                bank(PPB[0], 0, 1, pn=1), a(gcol, 1), a(A_ONE, 1),
                start=True, stop=True), waits=[(sV, rv)], inc=sT)
            e1 = op("scalar", lambda e: e.activation(
                out=a(gcol + 2, 1, pn=1), in_=bank(PPB[0], 0, 1, pn=1),
                func=Act.Copy), waits=[(sT, mm)], inc=sS)
            nelem = float(nblocks * 128 * width)
            w1_ = op("vector", lambda e, nelem=nelem: e.tensor_scalar(
                out=a(gcol + 3, 1, pn=1), in0=a(gcol + 2, 1, pn=1),
                scalar1=1.0 / nelem, scalar2=FFN_EPS, op0=Alu.mult,
                op1=Alu.add), waits=[(sS, e1)], inc=sV)
            w2_ = op("vector", lambda e: e.reciprocal(
                out=a(gcol + 4, 1, pn=1), in_=a(gcol + 3, 1, pn=1)),
                waits=[(sV, w1_)], inc=sV)
            w3_ = op("vector", lambda e: e.tensor_mul(
                out=a(gcol + 5, 1, pn=1), in0=a(gcol + 3, 1, pn=1),
                in1=a(gcol + 4, 1, pn=1)), waits=[(sV, w2_)], inc=sV)
            w4_ = op("vector", lambda e: e.tensor_scalar(
                out=a(gcol + 5, 1, pn=1), in0=a(gcol + 5, 1, pn=1),
                scalar1=-1.0, scalar2=2.0, op0=Alu.mult, op1=Alu.add),
                waits=[(sV, w3_)], inc=sV)
            v2 = op("vector", lambda e: e.tensor_mul(
                out=a(gcol + 4, 1, pn=1), in0=a(gcol + 4, 1, pn=1),
                in1=a(gcol + 5, 1, pn=1)), waits=[(sV, w4_)], inc=sV)
            mm2 = op("tensor", lambda e: e.matmul(
                bank(PPB[1], 0, 1), a(A_ONR, 128, pn=1), a(gcol + 4, 1, pn=1),
                start=True, stop=True), waits=[(sV, v2), (sV, v_onr)], inc=sT)
            e2 = op("scalar", lambda e: e.activation(
                out=a(gcol + 6, 1), in_=bank(PPB[1], 0, 1), func=Act.Copy),
                waits=[(sT, mm2)], inc=sS)
            return gcol + 6, e2

        qtm_tok = [None]
        qtm2_tok = [None]

        def mk_quant(rc, rev, g1c, g2c, width):
            def quant(m, d0):
                op("vector", lambda e: e.tensor_scalar(
                    out=a(g1c, width), in0=a(P_NAT, width), scalar1=a(rc, 1),
                    scalar2=0.5, op0=Alu.mult, op1=Alu.is_gt),
                    waits=[d0, (sS, rev), qtm_tok[0], qtm2_tok[0]], inc=sV)
                g2v = op("vector", lambda e: e.tensor_scalar(
                    out=a(g2c, width), in0=a(P_NAT, width), scalar1=a(rc, 1),
                    scalar2=-0.5, op0=Alu.mult, op1=Alu.is_lt), inc=sV)
                vq = op("vector", lambda e: e.tensor_sub(
                    out=ab(P_NLB, width), in0=a(g1c, width), in1=a(g2c, width)),
                    waits=[(sV, g2v)], inc=sV)
                qtm_tok[0] = (sV, vq)
                return vq
            return quant

        r1c, r1ev = gamma_pass(w_f1, MFF, DM, A_GAM)
        prep_weight(lambda m: w_f1[m * 128:(m + 1) * 128, :], MFF, KDM, f1T, DM,
                    quant=mk_quant(r1c, r1ev, A_AT, A_AT + 1024, DM))
        r2c, r2ev = gamma_pass(w_f2, MDM, FF, A_GAM)
        prep_weight(lambda m: w_f2[m * 128:(m + 1) * 128, :], MDM, MFF, f2T, FF,
                    quant=mk_quant(r2c, r2ev, A_DTU, A_AT, FF))
        v_rz = op("vector", lambda e: e.memset(a(A_XC, 2 * MDI * XCW), 0.0),
                  inc=sV)

        PREP_STORES = n_prep_stores[0]
        prep_store_toks = dall()

        conv_last, ffn1_last, ffn2_last = {}, {}, {}
        out_mm_last, ft_last, fev_last, ym_ev, a_ev = {}, {}, {}, {}, {}
        out_store_tok = {}
        xload, dtp_ev, bc_ev, bc_ev_l, rep_last, xnf_ready = {}, {}, {}, {}, {}, {}
        tr_last, xc_all, dt_all, relu_ev, ucv_all = {}, {}, {}, {}, {}
        y_last, z_ev = {}, {}
        ypend = {0: None, 1: None}
        scpend = {0: None, 1: None}
        ppend = {0: None, 1: None}
        wgrp = [0]
        ring_last = {}
        utap, dttap, ymtap = [None], [None], [None]

        def wload(scratch, t0, ntiles=G, extra=()):
            gi = wgrp[0]
            wgrp[0] += 1
            slot = gi % 4
            waits = list(extra)
            if gi >= 4:
                waits.append((sT, ring_last[gi - 4]))
            dv = dload(lambda e, scr=scratch, t0=t0, nt=ntiles, slot=slot:
                       e.dma_start(
                           out=ab(A_WR + slot * G * 64, nt * 128),
                           in_=scr[:, t0 * 128:(t0 + nt) * 128]),
                       waits=waits)
            return gi, slot, dv

        def wtile(slot, i):
            return ab(A_WR + slot * G * 64 + i * 64, 128)

        def xnfh(c, k):
            return ab(A_XNF + (c % 2) * 2 * KDM * 128 + k * 128, Tc)

        def xnfl(c, k):
            return ab(A_XNF + (c % 2) * 2 * KDM * 128 + KDM * 128 + k * 128,
                      Tc)

        def xch(i, t0, w):
            return ab(A_XC + i * XCW, 2 * XCW)[:, t0:t0 + w]

        def xcl(i, t0, w):
            return ab(A_XC + MDI * XCW + i * XCW, 2 * XCW)[:, t0:t0 + w]

        def uh(k):
            return ab(A_UH + k * 128, Tc)

        def ul(k):
            return ab(A_UL + k * 128, Tc)

        def yh(k):
            return ab(A_DTU + k * 128, Tc)

        def yl(k):
            return ab(A_DTU + 2048 + k * 128, Tc)

        def hh(k):
            return ab(A_Z + k * 128, Tc)

        def hl(k):
            return ab(A_DT + k * 128, Tc)

        def emit_xload(c, extra=()):
            xload[c] = dload(lambda e, c=c: e.dma_start(
                out=a(A_SX + (c % 2) * 2 * DM, 2 * DM).rearrange(
                    "p (t d) -> p t d", d=DM),
                in_=x_d[c * Tc:(c + 1) * Tc, :].rearrange(
                    "(t p) d -> p t d", p=128)),
                waits=list(extra))

        def emit_ln(c):
            sx = A_SX + (c % 2) * 2 * DM
            hw = [xload[c]]
            if c == 0:
                hw.append((sV, v_rz))
            if c > 0:
                hw.append((sT, conv_last[c - 1]))
            if c > 1:
                hw.append((sT, ft_last[c - 2]))
            op("vector", lambda e: e.tensor_copy(
                out=ab(A_XC, 4 * MDI * XCW).rearrange(
                    "p (m w) -> p m w", w=2 * XCW)[:, :, 0:3],
                in_=ab(A_XC, 4 * MDI * XCW).rearrange(
                    "p (m w) -> p m w", w=2 * XCW)[:, :, Tc:Tc + 3]),
                waits=hw, inc=sV)
            vv = None
            for tt in range(2):
                mv = op("vector", lambda e, tt=tt, sx=sx: e.reduce_sum(
                    out=a(A_STAT + 6 + tt, 1), in_=a(sx + tt * DM, DM),
                    axis=AxX), inc=sV)
                nv = op("vector", lambda e, tt=tt: e.tensor_scalar(
                    out=a(A_STAT + 8 + tt, 1), in0=a(A_STAT + 6 + tt, 1),
                    scalar1=-1.0 / DM, scalar2=None, op0=Alu.mult),
                    waits=[(sV, mv)], inc=sV)
                xw = [(sV, nv)]
                if tt == 0 and c > 0:
                    xw.append((sT, tr_last[c - 1]))
                xnv = op("vector", lambda e, tt=tt, sx=sx: e.tensor_scalar(
                    out=a(A_XNT + tt * DM, DM), in0=a(sx + tt * DM, DM),
                    scalar1=a(A_STAT + 8 + tt, 1), scalar2=None, op0=Alu.add),
                    waits=xw, inc=sV)
                sqv = op("vector", lambda e, tt=tt: e.tensor_mul(
                    out=a(A_AT + tt * DM, DM), in0=a(A_XNT + tt * DM, DM),
                    in1=a(A_XNT + tt * DM, DM)), waits=[(sV, xnv)], inc=sV)
                qv = op("vector", lambda e, tt=tt: e.reduce_sum(
                    out=a(A_STAT + 10 + tt, 1), in_=a(A_AT + tt * DM, DM),
                    axis=AxX), waits=[(sV, sqv)], inc=sV)
                vv = op("vector", lambda e, tt=tt: e.tensor_scalar(
                    out=a(A_STAT + tt, 1), in0=a(A_STAT + 10 + tt, 1),
                    scalar1=1.0 / DM, scalar2=LN_EPS, op0=Alu.mult,
                    op1=Alu.add), waits=[(sV, qv)], inc=sV)
            sq = op("scalar", lambda e: e.activation(
                out=a(A_STAT + 2, 2), in_=a(A_STAT, 2), func=Act.Sqrt),
                waits=[(sV, vv)], inc=sS)
            rcv = op("vector", lambda e: e.reciprocal(
                out=a(A_STAT + 4, 2), in_=a(A_STAT + 2, 2)), waits=[(sS, sq)],
                inc=sV)
            nmv = None
            for tt in range(2):
                nmv = op("vector", lambda e, tt=tt: e.tensor_scalar(
                    out=a(A_XNT + tt * DM, DM), in0=a(A_XNT + tt * DM, DM),
                    scalar1=a(A_STAT + 4 + tt, 1), scalar2=None, op0=Alu.mult),
                    waits=([(sV, rcv)] if tt == 0 else []), inc=sV)
            vh = op("vector", lambda e: e.tensor_copy(
                out=ab(A_AT, 2 * DM), in_=a(A_XNT, 2 * DM)),
                waits=[(sV, nmv)], inc=sV)
            vxn = op("vector", lambda e: e.tensor_sub(
                out=ab(A_AT + DM, 2 * DM), in0=a(A_XNT, 2 * DM),
                in1=ab(A_AT, 2 * DM)), waits=[(sV, vh)], inc=sV)
            a_ev[c] = vxn
            if c == TAPC:
                tap("xnt", A_XNT, 2 * DM, (sV, a_ev[c]))
                tap("stat", A_STAT, 16, (sV, rcv))

        pend = {}

        def acq(pool, cur_slice):
            s, t = pool.acquire()
            key = (id(pool), s)
            if t is None and key in pend:
                pend.pop(key)[1]()
                t = pool.tok[s]
                pool.tok[s] = None
            return s, t

        def defer(pool, s, sl, fn):
            pend[(id(pool), s)] = (sl, fn)

        def flush_pend(max_slice):
            for key in list(pend):
                if pend[key][0] <= max_slice:
                    pend.pop(key)[1]()

        def build_units(c):
            units = []

            def mk_tp(bt):
                def go(cur_slice=0):
                    hlv, tt = bt // 2, bt % 2
                    srcb = A_AT + hlv * DM + tt * 512
                    slot, tok = tp_pool.acquire()
                    last = None
                    for j in range(8):
                        last = op("tensor",
                                  lambda e, j=j, slot=slot, srcb=srcb: e.matmul(
                                      bankb(TPB[slot], j * 128, 128),
                                      ab(srcb + j * 64, 128), ab(A_IDB, 128),
                                      start=(j == 0), stop=(j == 7),
                                      is_transpose=True),
                                  waits=([(sV, a_ev[c]), tok] if j == 0
                                         else []),
                                  inc=(sT if j == 7 else None))
                    tr_last[c] = last
                    ew = [(sT, last)]
                    if bt == 0 and c > 1:
                        ew.append((sT, ffn1_last[c - 2]))
                    dstb = (A_XNF + (c % 2) * 2 * KDM * 128 + hlv * KDM * 128)
                    ev = op("scalar",
                            lambda e, slot=slot, dstb=dstb, tt=tt:
                            e.activation(
                                out=ab(dstb, KDM * Tc).rearrange(
                                    "p (dc t) -> p dc t", t=Tc)[
                                    :, :, tt * 128:(tt + 1) * 128],
                                in_=bankb(TPB[slot], 0, 1024).rearrange(
                                    "p (j t) -> p j t", t=128),
                                func=Act.Copy), waits=ew, inc=sS)
                    tp_pool.release(slot, sS, ev)
                    xnf_ready[c] = ev
                    return None
                return go
            for bt in range(4):
                units.append(mk_tp(bt))

            def mk_ip(m):
                def go(cur_slice=0):
                    gi, slot, dv = wload(ipT, m * 2 * KDM)
                    pslot, ptok = acq(pp_pool, cur_slice)
                    lastv = None
                    first = [True]
                    for xi, wj in ((0, 0), (1, 0), (0, 1)):
                        for k in range(KDM):
                            w0 = []
                            if first[0]:
                                w0 = [dv, ptok, (sS, xnf_ready[c])]
                                first[0] = False
                            xs = xnfh(c, k) if xi == 0 else xnfl(c, k)
                            st = (xi == 0 and wj == 0 and k == 0)
                            sp = (xi == 0 and wj == 1 and k == KDM - 1)
                            lastv = op("tensor",
                                       lambda e, k=k, wj=wj, xs=xs, slot=slot,
                                       pslot=pslot, st=st, sp=sp: e.matmul(
                                           bank(PPB[pslot], 0, Tc),
                                           wtile(slot, k * 2 + wj), xs,
                                           start=st, stop=sp),
                                       waits=w0, inc=(sT if sp else None))
                    ring_last[gi] = lastv

                    def evac():
                        ew = [(sT, lastv), (sV, a_ev[c])]
                        ev = op("scalar", lambda e, m=m, pslot=pslot:
                                e.activation(
                                    out=xch(m, 3, Tc),
                                    in_=bank(PPB[pslot], 0, Tc),
                                    func=Act.Copy), waits=ew, inc=sS)
                        lv = op("vector", lambda e, m=m, pslot=pslot:
                                e.tensor_sub(
                                    out=xcl(m, 3, Tc),
                                    in0=bank(PPB[pslot], 0, Tc),
                                    in1=xch(m, 3, Tc)), waits=[(sS, ev)],
                                inc=sV)
                        pp_pool.release(pslot, sV, lv)
                        xc_all[c] = lv
                    defer(pp_pool, pslot, cur_slice, evac)
                    return None
                return go
            for m in range(MDI):
                units.append(mk_ip(m))

            cvst = {}

            def mk_cv(m, tg, gg):
                def go(cur_slice=0):
                    gi, slot, dv = wload(
                        cvT, ((m * TAPS + tg) * MDI + gg * 8) * 2)
                    if tg == 0 and gg == 0:
                        cvst[m] = acq(pp_pool, cur_slice)
                    pslot, ptok = cvst[m]
                    first_mm = (tg == 0 and gg == 0)
                    lastv = None
                    first = [True]
                    for xi, wj in ((0, 0), (1, 0), (0, 1)):
                        for i8 in range(8):
                            i = gg * 8 + i8
                            w0 = []
                            if first[0]:
                                w0 = [dv]
                                if first_mm:
                                    w0.append(ptok)
                                    if m == 0:
                                        w0.append((sV, xc_all[c]))
                                first[0] = False
                            xs = (xch(i, tg, Tc) if xi == 0
                                  else xcl(i, tg, Tc))
                            st = (first_mm and xi == 0 and wj == 0
                                  and i8 == 0)
                            sp = (tg == TAPS - 1 and gg == 1 and xi == 0
                                  and wj == 1 and i8 == 7)
                            lastv = op("tensor",
                                       lambda e, i8=i8, wj=wj, xs=xs,
                                       slot=slot, pslot=pslot, st=st, sp=sp:
                                       e.matmul(
                                           bank(PPB[pslot], 0, Tc),
                                           wtile(slot, i8 * 2 + wj), xs,
                                           start=st, stop=sp),
                                       waits=w0,
                                       inc=(sT if (xi == 0 and wj == 1
                                                   and i8 == 7) else None))
                    ring_last[gi] = lastv
                    if not (tg == TAPS - 1 and gg == 1):
                        return None
                    conv_last[c] = lastv

                    def evac():
                        ew = [(sT, lastv)]
                        if m == 0 and c > 0:
                            ew.append((sV, dtp_ev[c - 1]))
                            if debug and c == TAPC + 1 and utap[0] is not None:
                                ew.append((sTAP, utap[0]))
                        ev = op("scalar", lambda e, m=m, pslot=pslot:
                                e.activation(
                                    out=uh(m), in_=bank(PPB[pslot], 0, Tc),
                                    func=Act.Identity, bias=a(A_CB + m, 1)),
                                waits=ew, inc=sS)
                        lv = op("vector", lambda e, m=m, pslot=pslot:
                                e.scalar_tensor_tensor(
                                    out=ul(m), in0=bank(PPB[pslot], 0, Tc),
                                    scalar=a(A_CB + m, 1), in1=uh(m),
                                    op0=Alu.add, op1=Alu.subtract),
                                waits=[(sS, ev)], inc=sV)
                        pp_pool.release(pslot, sV, lv)
                        ucv_all[c] = lv
                    defer(pp_pool, pslot, cur_slice, evac)
                    return None
                return go
            for m in range(MDI):
                for tg in range(TAPS):
                    for gg in range(2):
                        units.append(mk_cv(m, tg, gg))
            return units

        def unit_slice(i):
            if i < 4:
                return 0
            if i < 20:
                return 1 + (i - 4) * 3 // 16
            return 4 + (i - 20) * 12 // 128

        def emit_scan(c, units):
            ui = [0]

            def run_units(n):
                while ui[0] < len(units) and unit_slice(ui[0]) <= n:
                    units[ui[0]](n)
                    ui[0] += 1

            NJ = N * MDI
            rep_n, rslots = {}, {}
            b_ev, p_ev, sc_ev = {}, {}, {}
            eps_ev = {}

            def Bbuf(j):
                return A_B + (j % 2) * Tc

            def SSbuf(j):
                return A_SS + (j % 2) * Tc

            def Pbuf(j):
                return A_P + (j % 2) * Tc

            def emit_rep(n):
                rslot, rtok = rep_pool.acquire()
                rslots[n] = rslot
                first = [True]
                rl = None
                for half in range(2):
                    for hl in range(2):
                        w0 = []
                        if first[0]:
                            w0 = [rtok]
                            if n == 0:
                                w0 += [(sS, bc_ev[c]), (sV, bc_ev_l[c])]
                            first[0] = False
                        srow = A_SBC + hl * Tc + half * 128
                        st = (half == 0 and hl == 0)
                        sp = (half == 1 and hl == 1)
                        rl = op("tensor",
                                lambda e, rslot=rslot, half=half,
                                srow=srow, st=st, sp=sp: e.matmul(
                                    bank(REPB[rslot], half * Tc, Tc),
                                    ab(A_SEL + n * 64, 128),
                                    ab(srow, Tc),
                                    start=st, stop=sp),
                                waits=w0, inc=(sT if sp else None))
                rep_n[n] = rl
                rep_last[c] = rl
                if debug and c == TAPC and n == 0:
                    rv0 = op("vector", lambda e, rslot=rslot: e.tensor_copy(
                        out=a(A_OUT, 512), in_=bank(REPB[rslot], 0, 512)),
                        waits=[(sT, rl)], inc=sV)
                    tap("rep0", A_OUT, 512, (sV, rv0))

            def emit_b(j):
                n, cc = j // MDI, j % MDI
                w = [(sT, rep_n[n])] if cc == 0 else []
                if scpend[j % 2] is not None:
                    w.append((sV, scpend[j % 2]))
                b_ev[j] = op("vector",
                             lambda e, j=j, cc=cc, rslot=rslots[n]:
                             e.tensor_tensor(
                                 out=a(Bbuf(j), Tc),
                                 in0=a(A_DTU + cc * Tc, Tc),
                                 in1=bank(REPB[rslot], 0, Tc), op=Alu.mult),
                             waits=w, inc=sV)

            def emit_p(j):
                n = j // MDI
                w = [(sV, sc_ev[j])]
                if ypend[j % 2] is not None:
                    w.append((sG, ypend[j % 2]))
                p_ev[j] = op("vector",
                             lambda e, j=j, rslot=rslots[n]: e.tensor_tensor(
                                 out=a(Pbuf(j), Tc), in0=a(SSbuf(j), Tc),
                                 in1=bank(REPB[rslot], Tc, Tc), op=Alu.mult),
                             waits=w, inc=sV)
                ppend[j % 2] = p_ev[j]
                if j % MDI == MDI - 1:
                    rep_pool.release(rslots[n], sV, p_ev[j])

            def emit_y(j):
                cc = j % MDI
                yv = op("gpsimd", lambda e, j=j, cc=cc: e.tensor_tensor(
                    out=a(A_Y + cc * Tc, Tc), in0=a(Pbuf(j), Tc),
                    in1=a(A_Y + cc * Tc, Tc), op=Alu.add),
                    waits=[(sV, p_ev[j])], inc=sG)
                ypend[j % 2] = yv
                return yv

            emit_rep(0)
            emit_b(0)
            sc_val = None
            for j in range(NJ):
                n, cc = j // MDI, j % MDI
                if cc == 0:
                    flush_pend(n - 1)
                    run_units(n)
                if cc == MDI - 1 and n + 1 < N:
                    emit_rep(n + 1)
                dslot, dtok = da_pool.acquire()
                dw = [dtok]
                if j == 0:
                    dw.append((sV, dtp_ev[c]))
                op("scalar", lambda e, cc=cc, n=n, dslot=dslot: e.activation(
                    out=a(A_DA + dslot * Tc, Tc),
                    in_=a(A_DT + cc * Tc, Tc), func=Act.Exp,
                    scale=a(A_ASB + cc * N + n, 1)), waits=dw, inc=sS)
                eps_ev[j] = op("scalar", lambda e, j=j: e.activation(
                    out=a(Bbuf(j), Tc), in_=a(Bbuf(j), Tc),
                    func=Act.Identity, bias=a(A_EPS, 1)),
                    waits=[(sV, b_ev[j])], inc=sS)
                if j > 0:
                    op("scalar", lambda e, j=j: e.activation(
                        out=a(A_ST + (j - 1) % MDI * N + (j - 1) // MDI, 1),
                        in_=a(SSbuf(j - 1) + Tc - 1, 1), func=Act.Copy),
                        waits=[(sV, sc_ev[j - 1])])
                if j + 1 < NJ:
                    emit_b(j + 1)
                if j > 0:
                    emit_p(j - 1)
                scw = [(sS, eps_ev[j])]
                if ppend[j % 2] is not None:
                    scw.append((sV, ppend[j % 2]))
                sc_val = op("vector",
                            lambda e, j=j, cc=cc, n=n, dslot=dslot:
                            e.tensor_tensor_scan(
                                out=a(SSbuf(j), Tc),
                                data0=a(A_DA + dslot * Tc, Tc),
                                data1=a(Bbuf(j), Tc),
                                initial=a(A_ST + cc * N + n, 1),
                                op0=Alu.mult, op1=Alu.add),
                            waits=scw, inc=sV)
                sc_ev[j] = sc_val
                scpend[j % 2] = sc_val
                da_pool.release(dslot, sV, sc_val)
                if j > 0:
                    emit_y(j - 1)
            op("scalar", lambda e: e.activation(
                out=a(A_ST + (NJ - 1) % MDI * N + (NJ - 1) // MDI, 1),
                in_=a(SSbuf(NJ - 1) + Tc - 1, 1), func=Act.Copy),
                waits=[(sV, sc_ev[NJ - 1])])
            emit_p(NJ - 1)
            y_last[c] = emit_y(NJ - 1)
            while ui[0] < len(units):
                units[ui[0]](N)
                ui[0] += 1
            flush_pend(99)
            return y_last[c]

        def emit_tail(c):
            ym = None
            for cc in range(MDI):
                ym = op("vector", lambda e, cc=cc: e.scalar_tensor_tensor(
                    out=a(A_Y + cc * Tc, Tc), in0=a(A_Y + cc * Tc, Tc),
                    scalar=1.0, in1=a(A_Z + cc * Tc, Tc),
                    op0=Alu.mult, op1=Alu.mult),
                    waits=([(sG, y_last[c]), (sS, z_ev[c])] if cc == 0 else []),
                    inc=(sV if cc == MDI - 1 else None))
            ym_ev[c] = ym
            if c == TAPC:
                ymtap[0] = tap("ym", A_Y, MDI * Tc, (sV, ym_ev[c]))
            ysp = None
            for cc in range(MDI):
                yhv = op("vector", lambda e, cc=cc: e.tensor_copy(
                    out=yh(cc), in_=a(A_Y + cc * Tc, Tc)),
                    waits=([(sV, ym)] if cc == 0 else []), inc=sV)
                ysp = op("vector", lambda e, cc=cc: e.tensor_sub(
                    out=yl(cc), in0=a(A_Y + cc * Tc, Tc), in1=yh(cc)),
                    waits=[(sV, yhv)],
                    inc=(sV if cc == MDI - 1 else None))

            outev = {}
            for m in range(MDM):
                pslot, ptok = pp_pool.acquire()
                lastv = None
                for gg in range(2):
                    gi, slot, dv = wload(owT, (m * MDI + gg * 8) * 2)
                    first = [True]
                    for xi, wj in ((0, 0), (1, 0), (0, 1)):
                        for k8 in range(8):
                            k = gg * 8 + k8
                            w0 = []
                            if first[0]:
                                w0 = [dv]
                                if gg == 0:
                                    w0.append(ptok)
                                    w0.append((sV, ysp))
                                first[0] = False
                            xs = yh(k) if xi == 0 else yl(k)
                            st = (gg == 0 and xi == 0 and wj == 0 and k8 == 0)
                            sp = (gg == 1 and xi == 0 and wj == 1 and k8 == 7)
                            lastv = op("tensor",
                                       lambda e, k8=k8, wj=wj, xs=xs,
                                       slot=slot, pslot=pslot, st=st, sp=sp:
                                       e.matmul(
                                           bank(PPB[pslot], 0, Tc),
                                           wtile(slot, k8 * 2 + wj), xs,
                                           start=st, stop=sp),
                                       waits=w0,
                                       inc=(sT if (xi == 0 and wj == 1
                                                   and k8 == 7) else None))
                    ring_last[gi] = lastv
                out_mm_last[c] = lastv
                ew = [(sT, lastv)]
                if m == 0:
                    ew.append((sV, ym_ev[c]))
                    if (c + 1) in tr_last:
                        ew.append((sT, tr_last[c + 1]))
                outev[m] = op("scalar", lambda e, m=m, pslot=pslot:
                              e.activation(
                                  out=a(A_AT + m * Tc, Tc),
                                  in_=bank(PPB[pslot], 0, Tc),
                                  func=Act.Copy), waits=ew, inc=sS)
                pp_pool.release(pslot, sS, outev[m])
            at_tap = (tap("at", A_AT, MDM * Tc, (sS, outev[MDM - 1]))
                      if c == TAPC else None)

            relu_all = None
            for m in range(MFF):
                if m % 2 == 0:
                    gi, slot, dv = wload(f1T, m * KDM)
                pslot, ptok = pp_pool.acquire()
                lastv = None
                first = [True]
                for xi in range(2):
                    for k in range(KDM):
                        w0 = []
                        if first[0]:
                            w0 = [dv, ptok]
                            first[0] = False
                        xs = xnfh(c, k) if xi == 0 else xnfl(c, k)
                        st = (xi == 0 and k == 0)
                        sp = (xi == 1 and k == KDM - 1)
                        lastv = op("tensor",
                                   lambda e, m=m, k=k, xs=xs, slot=slot,
                                   pslot=pslot, st=st, sp=sp: e.matmul(
                                       bank(PPB[pslot], 0, Tc),
                                       wtile(slot, (m % 2) * KDM + k), xs,
                                       start=st, stop=sp),
                                   waits=w0, inc=(sT if sp else None))
                if m % 2 == 1:
                    ring_last[gi] = lastv
                ffn1_last[c] = lastv
                hslot, htok = hf_pool.acquire()
                ew = [(sT, lastv), htok]
                if m == 0:
                    ew.append((sV, ym_ev[c]))
                    if debug and c == TAPC and dttap[0] is not None:
                        ew.append((sTAP, dttap[0]))
                hf = op("scalar", lambda e, m=m, pslot=pslot, hslot=hslot:
                        e.activation(
                            out=a(HFC[hslot], Tc),
                            in_=bank(PPB[pslot], 0, Tc),
                            func=Act.Relu, bias=a(A_B1 + m, 1)),
                        waits=ew, inc=sS)
                pp_pool.release(pslot, sS, hf)
                hcv = op("scalar", lambda e, m=m, hslot=hslot: e.activation(
                    out=hh(m), in_=a(HFC[hslot], Tc), func=Act.Copy),
                    waits=[(sS, hf)], inc=sS)
                relu_all = op("vector", lambda e, m=m, hslot=hslot:
                              e.tensor_sub(
                                  out=hl(m), in0=a(HFC[hslot], Tc),
                                  in1=hh(m)), waits=[(sS, hcv)], inc=sV)
                hf_pool.release(hslot, sV, relu_all)
            relu_ev[c] = relu_all

            fe = {}
            for m in range(MDM):
                pslot, ptok = pp_pool.acquire()
                lastv = None
                for gg in range(2):
                    gi, slot, dv = wload(f2T, m * MFF + gg * G)
                    first = [True]
                    for xi in range(2):
                        for kk in range(G):
                            k = gg * G + kk
                            w0 = []
                            if first[0]:
                                w0 = [dv]
                                if gg == 0:
                                    w0.append(ptok)
                                    if m == 0:
                                        w0.append((sV, relu_all))
                                first[0] = False
                            xs = hh(k) if xi == 0 else hl(k)
                            st = (gg == 0 and xi == 0 and kk == 0)
                            sp = (gg == 1 and xi == 1 and kk == G - 1)
                            lastv = op("tensor",
                                       lambda e, kk=kk, xs=xs, slot=slot,
                                       pslot=pslot, st=st, sp=sp: e.matmul(
                                           bank(PPB[pslot], 0, Tc),
                                           wtile(slot, kk), xs,
                                           start=st, stop=sp),
                                       waits=w0,
                                       inc=(sT if (xi == 1 and kk == G - 1)
                                            else None))
                    ring_last[gi] = lastv
                ffn2_last[c] = lastv
                vw = [(sT, lastv), (sS, outev[m])]
                if c == TAPC and m == 0 and at_tap is not None:
                    vw.append((sTAP, at_tap))
                fe[m] = op("vector",
                           lambda e, m=m, pslot=pslot: e.scalar_tensor_tensor(
                               out=a(A_AT + m * Tc, Tc),
                               in0=bank(PPB[pslot], 0, Tc),
                               scalar=a(A_B2 + m, 1), in1=a(A_AT + m * Tc, Tc),
                               op0=Alu.add, op1=Alu.add), waits=vw, inc=sV)
                pp_pool.release(pslot, sV, fe[m])
            if c == TAPC:
                tap("sf", A_AT, MDM * Tc, (sV, fe[MDM - 1]))

            sx = A_SX + (c % 2) * 2 * DM
            for bt in range(4):
                tt, dq = bt // 2, bt % 2
                slot, tok = tp_pool.acquire()
                last = None
                for j in range(4):
                    dc = dq * 4 + j
                    w0 = []
                    if j == 0:
                        w0 = [tok, (sV, fe[dq * 4 + 3])]
                    last = op("tensor",
                              lambda e, tt=tt, dc=dc, j=j, slot=slot: e.matmul(
                                  bank(TPB[slot], j * 128, 128),
                                  a(A_AT + dc * Tc + tt * 128, 128),
                                  a(A_ID, 128),
                                  start=(j == 0), stop=(j == 3),
                                  is_transpose=True),
                              waits=w0, inc=(sT if j == 3 else None))
                ft_last[c] = last
                vw = [(sT, last)]
                if bt == 0 and c > 0:
                    vw.append(out_store_tok[c - 1])
                fev_last[c] = op("vector",
                                 lambda e, tt=tt, dq=dq, slot=slot,
                                 sx=sx: e.tensor_add(
                                     out=a(A_OUT + tt * DM + dq * 512, 512),
                                     in0=bank(TPB[slot], 0, 512),
                                     in1=a(sx + tt * DM + dq * 512, 512)),
                                 waits=vw, inc=sV)
                tp_pool.release(slot, sV, fev_last[c])

            out_store_tok[c] = dstore(lambda e, c=c: e.dma_start(
                out=out_d[c * Tc:(c + 1) * Tc, :].rearrange(
                    "(t p) d -> p t d", p=128),
                in_=a(A_OUT, 2 * DM).rearrange("p (t d) -> p t d", d=DM)),
                waits=[(sV, fev_last[c])])

        def emit_front_b(c, prolog=False):
            zev = None
            for m in range(MDI, MIP):
                gi, slot, dv = wload(ipT, m * 2 * KDM)
                pslot, ptok = pp_pool.acquire()
                lastv = None
                first = [True]
                for xi, wj in ((0, 0), (1, 0), (0, 1)):
                    for k in range(KDM):
                        w0 = []
                        if first[0]:
                            w0 = [dv, ptok]
                            if m == MDI and prolog:
                                w0.append((sS, xnf_ready[c]))
                            first[0] = False
                        xs = xnfh(c, k) if xi == 0 else xnfl(c, k)
                        st = (xi == 0 and wj == 0 and k == 0)
                        sp = (xi == 0 and wj == 1 and k == KDM - 1)
                        lastv = op("tensor",
                                   lambda e, k=k, wj=wj, xs=xs, slot=slot,
                                   pslot=pslot, st=st, sp=sp: e.matmul(
                                       bank(PPB[pslot], 0, Tc),
                                       wtile(slot, k * 2 + wj), xs,
                                       start=st, stop=sp),
                                   waits=w0, inc=(sT if sp else None))
                ring_last[gi] = lastv
                ew = [(sT, lastv)]
                if m == MDI and not prolog:
                    ew.append((sT, ffn2_last[c - 1]))
                zev = op("scalar", lambda e, m=m, pslot=pslot: e.activation(
                    out=a(A_Z + (m - MDI) * Tc, Tc),
                    in_=bank(PPB[pslot], 0, Tc), func=Act.Silu),
                    waits=ew, inc=sS)
                pp_pool.release(pslot, sS, zev)
            z_ev[c] = zev
            if c == TAPC:
                tap("z", A_Z, MDI * Tc, (sS, zev))

            dt_ev = None
            for m in range(MDI):
                pslot, ptok = pp_pool.acquire()
                lastv = None
                for gg in range(2):
                    gi, slot, dv = wload(dtT, (m * MDI + gg * 8) * 2)
                    first = [True]
                    for xi, wj in ((0, 0), (1, 0), (0, 1)):
                        for k8 in range(8):
                            k = gg * 8 + k8
                            w0 = []
                            if first[0]:
                                w0 = [dv]
                                if gg == 0:
                                    w0.append(ptok)
                                    if m == 0:
                                        w0.append((sV, ucv_all[c]))
                                first[0] = False
                            xs = uh(k) if xi == 0 else ul(k)
                            st = (gg == 0 and xi == 0 and wj == 0
                                  and k8 == 0)
                            sp = (gg == 1 and xi == 0 and wj == 1
                                  and k8 == 7)
                            lastv = op("tensor",
                                       lambda e, k8=k8, wj=wj, xs=xs,
                                       slot=slot, pslot=pslot, st=st, sp=sp:
                                       e.matmul(
                                           bank(PPB[pslot], 0, Tc),
                                           wtile(slot, k8 * 2 + wj), xs,
                                           start=st, stop=sp),
                                       waits=w0,
                                       inc=(sT if (xi == 0 and wj == 1
                                                   and k8 == 7) else None))
                    ring_last[gi] = lastv
                ew = [(sT, lastv)]
                if m == 0 and not prolog:
                    ew.append((sT, ffn2_last[c - 1]))
                dt_ev = op("scalar", lambda e, m=m, pslot=pslot: e.activation(
                    out=a(A_DT + m * Tc, Tc), in_=bank(PPB[pslot], 0, Tc),
                    func=Act.Sigmoid, bias=a(A_DB + m, 1)),
                    waits=ew, inc=sS)
                pp_pool.release(pslot, sS, dt_ev)
            dt_all[c] = dt_ev

            pslot, ptok = pp_pool.acquire()
            lastv = None
            first = [True]
            for half in range(2):
                for xi, wj in ((0, 0), (1, 0), (0, 1)):
                    for k in range(MDI):
                        w0 = []
                        if first[0]:
                            w0 = [ptok]
                            first[0] = False
                        wt = ab(A_BCW + ((half * 2 + wj) * MDI + k) * 8, N)
                        xs = uh(k) if xi == 0 else ul(k)
                        st = (half == 0 and xi == 0 and wj == 0 and k == 0)
                        sp = (half == 1 and xi == 0 and wj == 1
                              and k == MDI - 1)
                        lastv = op("tensor",
                                   lambda e, wt=wt, xs=xs, half=half,
                                   pslot=pslot, st=st, sp=sp: e.matmul(
                                       bank(PPB[pslot], half * Tc, Tc, pn=N),
                                       wt, xs, start=st, stop=sp),
                                   waits=w0, inc=(sT if sp else None))
            ew = [(sT, lastv)]
            if not prolog:
                ew.append((sT, rep_last[c - 1]))
            bc_ev[c] = op("scalar", lambda e, pslot=pslot: e.activation(
                out=ab(A_SBC, 2 * Tc, pn=N),
                in_=bank(PPB[pslot], 0, 2 * Tc, pn=N),
                func=Act.Copy), waits=ew, inc=sS)
            bc_ev_l[c] = op("vector", lambda e, pslot=pslot: e.tensor_sub(
                out=ab(A_SBC + Tc, 2 * Tc, pn=N),
                in0=bank(PPB[pslot], 0, 2 * Tc, pn=N),
                in1=ab(A_SBC, 2 * Tc, pn=N)),
                waits=[(sS, bc_ev[c])], inc=sV)
            pp_pool.release(pslot, sV, bc_ev_l[c])
            if c == TAPC:
                tap("sbch", A_SBC, Tc, (sV, bc_ev_l[c]), pn=N, bf=True)
                tap("sbcl", A_SBC + Tc, Tc, (sV, bc_ev_l[c]), pn=N, bf=True)

            clw = [(sS, dt_all[c])]
            if not prolog:
                clw.append((sT, out_mm_last[c - 1]))
            cl = op("vector", lambda e: e.tensor_scalar(
                out=a(A_DT, MDI * Tc), in0=a(A_DT, MDI * Tc), scalar1=DT_MIN,
                scalar2=1.0, op0=Alu.max, op1=Alu.min), waits=clw, inc=sV)
            dtp = None
            for cc in range(MDI):
                op("vector", lambda e, cc=cc: e.tensor_tensor(
                    out=a(A_B, Tc), in0=a(A_DT + cc * Tc, Tc), in1=uh(cc),
                    op=Alu.mult), waits=([(sV, cl)] if cc == 0 else []))
                duv = op("vector", lambda e, cc=cc: e.tensor_tensor(
                    out=a(A_SS, Tc), in0=a(A_DT + cc * Tc, Tc), in1=ul(cc),
                    op=Alu.mult), inc=sV)
                op("vector", lambda e, cc=cc: e.tensor_add(
                    out=a(A_DTU + cc * Tc, Tc), in0=a(A_B, Tc),
                    in1=a(A_SS, Tc)), waits=[(sV, duv)])
                yiv = op("vector", lambda e, cc=cc: e.tensor_scalar(
                    out=a(A_Y + cc * Tc, Tc), in0=uh(cc),
                    scalar1=a(A_DV + cc, 1), scalar2=None, op0=Alu.mult),
                    inc=sV)
                dtp = op("vector", lambda e, cc=cc: e.scalar_tensor_tensor(
                    out=a(A_Y + cc * Tc, Tc), in0=ul(cc),
                    scalar=a(A_DV + cc, 1), in1=a(A_Y + cc * Tc, Tc),
                    op0=Alu.mult, op1=Alu.add), waits=[(sV, yiv)],
                    inc=(sV if cc == MDI - 1 else None))
            dtp_ev[c] = dtp
            if c == TAPC:
                dttap[0] = tap("dt", A_DT, MDI * Tc, (sV, dtp_ev[c]))
                tap("dtu", A_DTU, MDI * Tc, (sV, dtp_ev[c]))

        emit_xload(0, extra=prep_store_toks)
        if NCH > 1:
            emit_xload(1)
        emit_ln(0)
        for iu, u in enumerate(build_units(0)):
            if iu == 20:
                flush_pend(99)
            u(0)
        flush_pend(99)
        if debug and TAPC == 0:
            utap[0] = tap("u", A_UH, MDI * 128, (sV, ucv_all[0]), bf=True)
            tap("ulo", A_UL, MDI * 128, (sV, ucv_all[0]), bf=True)
            tap("xch", A_XC, MDI * XCW, (sV, xc_all[0]), bf=True)
            tap("xcl", A_XC + MDI * XCW, MDI * XCW, (sV, xc_all[0]), bf=True)
        emit_front_b(0, prolog=True)

        for c in range(NCH):
            if c + 1 < NCH:
                emit_ln(c + 1)
                units = build_units(c + 1)
            else:
                units = []
            sch.mark("tensor", f"sc{c:02d}")
            sl = emit_scan(c, units)
            if c == TAPC:
                tap("y", A_Y, MDI * Tc, (sG, sl))
            sch.mark("tensor", f"tl{c:02d}")
            emit_tail(c)
            if c + 1 < NCH:
                sch.mark("tensor", f"fb{c + 1:02d}")
                emit_front_b(c + 1)
                if debug and c + 1 == TAPC:
                    utap[0] = tap("u", A_UH, MDI * 128, (sV, ucv_all[c + 1]),
                                  bf=True)
                    tap("ulo", A_UL, MDI * 128, (sV, ucv_all[c + 1]), bf=True)
            if c + 2 < NCH:
                emit_xload(c + 2, extra=[(sV, fev_last[c])])

        op("gpsimd", None, waits=dall())

        used = set()
        for _eng in sch.ops:
            for _it in sch.ops[_eng]:
                if isinstance(_it, str):
                    continue
                for _s, _v in _it[1]:
                    used.add((id(_s), _v))
        oldc, newc, vmap = {}, {}, {}
        for _eng in ("tensor", "vector", "scalar", "sync", "gpsimd"):
            for oi, _it in enumerate(sch.ops[_eng]):
                if isinstance(_it, str):
                    continue
                _fn, _w, _i, _ib = _it
                if _i is None:
                    continue
                k = id(_i)
                ov = oldc.get(k, 0) + _ib
                oldc[k] = ov
                if (k, ov) in used:
                    nv = newc.get(k, 0) + _ib
                    newc[k] = nv
                    vmap[(k, ov)] = nv
                else:
                    sch.ops[_eng][oi] = (_fn, _w, None, _ib)

        def replay(eng):
            items = sch.ops[eng]

            def body(e, items=items):
                seen = {}
                stk = ExitStack()
                for it in items:
                    if isinstance(it, str):
                        stk.close()
                        stk = ExitStack()
                        stk.enter_context(nc.named_scope(it))
                        continue
                    fn, waits, inc, inc_by = it
                    for s, v in waits:
                        v = vmap[(id(s), v)]
                        if seen.get(id(s), 0) < v:
                            e.wait_ge(s, v)
                            seen[id(s)] = v
                    if fn is None:
                        continue
                    ins = fn(e)
                    if inc is not None:
                        ins.then_inc(inc, inc_by)
                stk.close()
            return body

        block.sync(replay("sync"))
        block.tensor(replay("tensor"))
        block.vector(replay("vector"))
        block.scalar(replay("scalar"))
        block.gpsimd(replay("gpsimd"))

    return nc


_NC_CACHE = {}


def kernel(L0_attn_x, L0_attn_in_proj_w, L0_attn_conv_w, L0_attn_conv_b,
           L0_attn_dt_w, L0_attn_dt_b, L0_attn_A_log, L0_attn_D, L0_attn_B_w,
           L0_attn_C_w, L0_attn_out_w, L0_ffn_W1, L0_ffn_b1, L0_ffn_W2,
           L0_ffn_b2):
    S = L0_attn_x.shape[1]
    if S not in _NC_CACHE:
        _NC_CACHE[S] = build_program(S)
    nc = _NC_CACHE[S]
    w = {
        "in_proj_w": np.ascontiguousarray(L0_attn_in_proj_w, np.float32),
        "conv_w": np.ascontiguousarray(L0_attn_conv_w, np.float32),
        "conv_b": np.ascontiguousarray(L0_attn_conv_b, np.float32),
        "dt_w": np.ascontiguousarray(L0_attn_dt_w, np.float32),
        "dt_b": np.ascontiguousarray(L0_attn_dt_b, np.float32),
        "A_log": np.ascontiguousarray(L0_attn_A_log, np.float32),
        "Dv": np.ascontiguousarray(L0_attn_D, np.float32),
        "B_w": np.ascontiguousarray(L0_attn_B_w, np.float32),
        "C_w": np.ascontiguousarray(L0_attn_C_w, np.float32),
        "out_w": np.ascontiguousarray(L0_attn_out_w, np.float32),
        "W1": np.ascontiguousarray(L0_ffn_W1, np.float32),
        "b1": np.ascontiguousarray(L0_ffn_b1, np.float32),
        "W2": np.ascontiguousarray(L0_ffn_W2, np.float32),
        "b2": np.ascontiguousarray(L0_ffn_b2, np.float32),
    }
    nb = L0_attn_x.shape[0]
    in_maps = [dict(w, x=np.ascontiguousarray(L0_attn_x[b], np.float32))
               for b in range(nb)]
    res = run_bass_kernel_spmd(nc, in_maps, core_ids=list(range(nb)))
    return np.stack([r["out"] for r in res.results], axis=0)
